# version 09
# baseline (speedup 1.0000x reference)
import contextlib
import numpy as np
import ml_dtypes
import concourse.bass as bass
import concourse.mybir as mybir
from concourse.bass_utils import run_bass_kernel_spmd

F32 = mybir.dt.float32
BF16 = mybir.dt.bfloat16
AX = mybir.AxisListType.X
OP = mybir.AluOpType
AF = mybir.ActivationFunctionType

P = 128
D = 1024
KD = 8
TPC = 2048
NTT = 2
TT = 1024
TH = 512
E = 8
NE = 10
H = 768
JH = 6
SCALE = 1.0 / 32.0
EPS = 1e-5

ENGS = ("T", "V", "S", "G", "P")


class Sem:
    def __init__(self, handle, name):
        self.h = handle
        self.name = name
        self.c = 0


class Script:
    def __init__(self):
        self.ops = {e: [] for e in ENGS}

    def emit(self, eng, fn, waits=(), inc=None):
        waits = tuple((s, v) for t_ in waits if t_ is not None
                      for (s, v) in [t_] if s is not None)
        tok = None
        if inc is not None:
            sem, n = inc
            sem.c += n
            tok = (sem, sem.c)
            inc = (sem, n)
        self.ops[eng].append((fn, waits, inc))
        return tok

    def check(self):
        ptr = {e: 0 for e in ENGS}
        val = {}
        progress = True
        while progress:
            progress = False
            for e in ENGS:
                while ptr[e] < len(self.ops[e]):
                    fn, waits, inc = self.ops[e][ptr[e]]
                    if any(val.get(s.name, 0) < v for s, v in waits):
                        break
                    if inc is not None:
                        val[inc[0].name] = val.get(inc[0].name, 0) + inc[1]
                    ptr[e] += 1
                    progress = True
        stuck = [e for e in ENGS if ptr[e] < len(self.ops[e])]
        if stuck:
            det = []
            for e in stuck:
                fn, waits, inc = self.ops[e][ptr[e]]
                blk = [(s.name, v, val.get(s.name, 0)) for s, v in waits
                       if val.get(s.name, 0) < v]
                det.append(f"{e}[{ptr[e]}/{len(self.ops[e])}] blocked on {blk}")
            raise RuntimeError("DEADLOCK: " + "; ".join(det))


def build_program(debug=False, att_only=False, att_level=None):
    nc = bass.Bass("TRN2", target_bir_lowering=False)

    xT = nc.dram_tensor("xT", [D, TPC], F32, kind="ExternalInput")
    w1t = nc.dram_tensor("w1t", [E, D, 2 * H], F32, kind="ExternalInput")
    w2t = nc.dram_tensor("w2t", [E, H, D], F32, kind="ExternalInput")
    ws1a = nc.dram_tensor("ws1a", [D, 2 * H], F32, kind="ExternalInput")
    ws1b = nc.dram_tensor("ws1b", [D, 2 * H], F32, kind="ExternalInput")
    ws2a = nc.dram_tensor("ws2a", [H, D], F32, kind="ExternalInput")
    ws2b = nc.dram_tensor("ws2b", [H, D], F32, kind="ExternalInput")
    wqn = nc.dram_tensor("wqn", [D, D], F32, kind="ExternalInput")
    wvn = nc.dram_tensor("wvn", [D, D], F32, kind="ExternalInput")
    wkTd = nc.dram_tensor("wkTd", [D, D], F32, kind="ExternalInput")
    wvTd = nc.dram_tensor("wvTd", [D, D], F32, kind="ExternalInput")
    wconv = nc.dram_tensor("wconv", [1, D], F32, kind="ExternalInput")
    gwT = nc.dram_tensor("gwT", [D, E], F32, kind="ExternalInput")
    id32 = nc.dram_tensor("id32", [P, P], F32, kind="ExternalInput")
    idbf = nc.dram_tensor("idbf", [P, P], BF16, kind="ExternalInput")
    onec = nc.dram_tensor("onec", [P, 1], BF16, kind="ExternalInput")
    oner = nc.dram_tensor("oner", [1, P], BF16, kind="ExternalInput")
    out_fm = nc.dram_tensor("out_fm", [D, TPC], F32, kind="ExternalOutput")
    xtp = nc.dram_tensor("xtp", [4 * TPC, D], F32, kind="ExternalInput")
    oner32 = nc.dram_tensor("oner32", [1, P], F32, kind="ExternalInput")
    xtpb = nc.dram_tensor("xtpb", [4 * TPC, D], BF16)
    drbf = nc.dram_tensor("drbf", [1, D], BF16)
    drow = nc.dram_tensor("drow", [1, D], F32)
    dag = nc.dram_tensor("dag", [1, TPC], BF16)
    dfb = [nc.dram_tensor("dfb0", [1, TH], BF16),
           nc.dram_tensor("dfb1", [1, TH], BF16)]
    dwbd = [nc.dram_tensor("dwb0", [1, TT], BF16),
            nc.dram_tensor("dwb1", [1, TT], BF16)]

    dbg = {}
    if globals().get('DBG2', False):
        dbg["d2_tiny"] = nc.dram_tensor("d2_tiny", [1, 16], F32, kind="ExternalOutput")
        dbg["d2_m2"] = nc.dram_tensor("d2_m2", [P, KD], F32, kind="ExternalOutput")
        dbg["d2_m1"] = nc.dram_tensor("d2_m1", [P, KD], F32, kind="ExternalOutput")
        dbg["d2_av"] = nc.dram_tensor("d2_av", [P, KD], F32, kind="ExternalOutput")
    if debug:
        for name, shape in [
            ("dbg_u", [P, KD]), ("dbg_ag", [1, TPC]), ("dbg_m1", [P, KD]),
            ("dbg_aa", [1, D]), ("dbg_w2", [1, D]), ("dbg_s", [1, TPC]),
            ("dbg_av", [P, KD]), ("dbg_x1", [P, KD, TT]),
            ("dbg_wd", [P, KD, E]), ("dbg_gl", [P, KD, E]), ("dbg_gp", [P, KD, E]),
            ("dbg_r2", [P, KD, TT]),
            ("dbg_x1b", [P, KD, TT]), ("dbg_r2b", [P, KD, TT]),
            ("dbg_wdb", [P, KD, E]),
        ]:
            dbg[name] = nc.dram_tensor(name, shape, F32, kind="ExternalOutput")
        dbg["dbg_h0"] = nc.dram_tensor("dbg_h0", [P, TH], BF16, kind="ExternalOutput") \
            if debug else None

    ctx = contextlib.ExitStack()
    with ctx:
        sbt = lambda name, shape, dt: ctx.enter_context(nc.sbuf_tensor(name, shape, dt))
        pst = lambda name: ctx.enter_context(nc.psum_tensor(name, [P, TH], F32))
        mks = lambda name: Sem(ctx.enter_context(nc.semaphore(name)), name)

        xtq = sbt("xtq", [P, 4, 4, TT], BF16)
        acol = sbt("acol", [P, 16], F32)
        bcol = sbt("bcol", [P, 16], BF16)
        xf = sbt("xf", [P, KD, TT], F32)
        x1bf = sbt("x1bf", [P, KD, TT], BF16)
        w1b = sbt("w1b", [P, 2, KD, 2 * H], BF16)
        w2b = sbt("w2b", [P, 2, JH, D], BF16)
        actb = sbt("actb", [P, JH, TT], BF16)
        wbc = sbt("wbc", [P, 2, TT], BF16)
        attbc = sbt("attbc", [P, TPC], BF16)
        silu_s = sbt("silu_s", [P, 2, TH], BF16)
        yw_s = sbt("yw_s", [P, 2, TH], BF16)
        rbf = sbt("rbf", [P, 2, TH], BF16)
        sqs = sbt("sqs", [P, 2, TH], BF16)
        gsc = sbt("gsc", [P, KD, E], F32)
        wdtm = sbt("wdtm", [P, KD, E], F32)
        mxt = sbt("mxt", [P, KD, 1], F32)
        mxt2 = sbt("mxt2", [P, KD, 1], F32)
        wfm = sbt("wfm", [E, TT], BF16)
        row32 = sbt("row32", [1, 4096], F32)
        rowbf = sbt("rowbf", [1, TPC], BF16)
        m8a = sbt("m8a", [KD + 1, P], F32)
        m8bf = sbt("m8bf", [KD, P], BF16)
        cols = sbt("cols", [P, 9 * KD], F32)
        colbf = sbt("colbf", [P, 8 * KD], BF16)
        tiny = sbt("tiny", [1, 16], F32)
        idsb = sbt("idsb", [P, P], F32)
        idbsb = sbt("idbsb", [P, P], BF16)
        onecsb = sbt("onecsb", [P, 1], BF16)
        onersb = sbt("onersb", [1, P], BF16)
        onersb32 = sbt("onersb32", [1, P], F32)
        gwsb = sbt("gwsb", [P, KD, E], F32)
        dbgst = sbt("dbgst", [P, TH], BF16) if debug else None

        C_U, C_M1, C_M2, C_AA, C_W2, C_AV, C_WCV, C_TMP, C_ED = \
            0, 1, 2, 3, 4, 5, 6, 7, 8
        csl = lambda w, f=None: cols[:, w * KD:(w + 1) * KD] if f is None \
            else cols[:, w * KD + f: w * KD + f + 1]
        cbsl = lambda w, f=None: colbf[:, w * KD:(w + 1) * KD] if f is None \
            else colbf[:, w * KD + f: w * KD + f + 1]

        seg = lambda a, n: row32[:, a:a + n]
        glsb = attbc.ap().bitcast(F32)[0:E, :]

        pb = [pst(f"pb{i}") for i in range(8)]

        sT, sV, sS = mks("sT"), mks("sV"), mks("sS")
        dP, dG, dO = mks("dP"), mks("dG"), mks("dO")
        dP2, dO2 = mks("dP2"), mks("dO2")
        dX0, dX1 = mks("dX0"), mks("dX1")
        dE = [mks(f"dE{i}") for i in range(4)]
        dXB, dQ = mks("dXB"), mks("dQ")
        dAB = mks("dAB")
        dFB = [mks("dFB0"), mks("dFB1")]
        dWB = [mks("dWB0"), mks("dWB1")]
        dF = [mks(f"dF{i}") for i in range(8)]
        dW1 = [mks("dW10"), mks("dW11")]
        dW2 = [mks("dW20"), mks("dW21")]
        S_ = Script()

        def T(fn, waits=(), inc=False):
            return S_.emit("T", fn, waits, (sT, 1) if inc else None)

        def V(fn, waits=(), inc=True):
            return S_.emit("V", fn, waits, (sV, 1) if inc else None)

        def SS(fn, waits=(), inc=True):
            return S_.emit("S", fn, waits, (sS, 1) if inc else None)

        def dma_sp(dst, src, waits=()):
            return S_.emit("P", lambda e, d=dst, s=src: e.dma_start(out=d, in_=s),
                           waits, (dP, 16))

        def dma_out(dst, src, waits=()):
            return S_.emit("P", lambda e, d=dst, s=src: e.dma_start(out=d, in_=s),
                           waits, (dO, 16))

        def dma_sp2(dst, src, waits=()):
            return S_.emit("P", lambda e, d=dst, s=src: e.dma_start(out=d, in_=s),
                           waits, (dP2, 16))

        def dma_out2(dst, src, waits=()):
            return S_.emit("P", lambda e, d=dst, s=src: e.dma_start(out=d, in_=s),
                           waits, (dO2, 16))

        def dma_s(dst, src, sem, waits=()):
            return S_.emit("S", lambda e, d=dst, s=src: e.dma_start(out=d, in_=s),
                           waits, (sem, 16))

        def bcast_dma(dst_2d, src_row, dram_t, sem, waits=()):
            n = src_row.shape[-1]
            t1_ = dma_s(dram_t.ap()[0:1, 0:n], src_row, sem, waits=waits)
            return dma_s(dst_2d, dram_t.ap()[0:1, 0:n].broadcast_to((P, n)),
                         sem, waits=[t1_])

        def dma_g(dst, src, waits=(), sem=None):
            return S_.emit("G", lambda e, d=dst, s=src: e.dma_start(out=d, in_=s),
                           waits, (sem if sem is not None else dG, 16))

        bank_tok = [None] * 8

        def bopen(i):
            w = bank_tok[i]
            bank_tok[i] = None
            return [w] if w is not None else []

        def bclose(i, tok):
            bank_tok[i] = tok

        t_id32 = dma_sp(idsb[:, :], id32[:, :])
        t_idbf = dma_sp(idbsb[:, :], idbf[:, :])
        t_onec = dma_sp(onecsb[:, :], onec[:, :])
        t_oner = dma_sp(onersb[:, :], oner[:, :])
        t_oner32 = dma_sp(onersb32[:, :], oner32[:, :])
        t_gw = dma_sp(gwsb[:, :, :], gwT.ap().rearrange("(k p) e -> p k e", p=P))
        t_wcv = dma_sp(m8a[0:KD, :], wconv.ap().rearrange("o (k p) -> (o k) p", p=P))
        xr = xT.ap().rearrange("(k p) t -> p k t", p=P)
        t_xf = []
        qkv_slot = w1b[:, 1, :, 0:D]
        t_wq = dma_g(qkv_slot, wqn.ap().rearrange("(k p) d -> p k d", p=P), sem=dQ)

        seq = [(tt, i) for tt in range(NTT) for i in range(NE)]
        w1_par = {}
        w2_par = {}
        for n, key in enumerate(seq):
            w1_par[key] = n % 2
            w2_par[key] = n % 2

        def w1_src(i):
            if i < E:
                return w1t.ap()[i].rearrange("(k p) h -> p k h", p=P)
            return (ws1a if i == E else ws1b).ap().rearrange("(k p) h -> p k h", p=P)

        def w2_src(i):
            if i < E:
                return w2t.ap()[i].rearrange("(j p) d -> p j d", p=P)
            return (ws2a if i == E else ws2b).ap().rearrange("(j p) d -> p j d", p=P)

        t_w1 = {}
        t_w2 = {}
        w1_free = {0: None, 1: None}
        w2_free = {0: None, 1: None}

        def emit_w1_load(key):
            if key in t_w1:
                return
            par = w1_par[key]
            t_w1[key] = dma_g(w1b[:, par, :, :], w1_src(key[1]),
                              waits=[w1_free[par]] if w1_free[par] else [],
                              sem=dW1[par])

        def emit_w2_load(key):
            if key in t_w2:
                return
            par = w2_par[key]
            t_w2[key] = dma_g(w2b[:, par, :, :], w2_src(key[1]),
                              waits=[w2_free[par]] if w2_free[par] else [],
                              sem=dW2[par])


        def row_matvec(out_row, lhs_col_bf, rhs3, pre_waits, scale_ap=None):
            toks = []
            last = None
            for hf in range(2):
                w0 = bopen(7)
                for k in range(KD):
                    w = list(pre_waits) + w0 if (hf == 0 and k == 0) else \
                        (w0 if k == 0 else [])
                    last = T(lambda e, hf=hf, k=k: e.matmul(
                        pb[7][0:1, :], lhs_col_bf[:, k:k + 1],
                        rhs3[:, k, TH * hf:TH * (hf + 1)],
                        start=(k == 0), stop=(k == KD - 1), skip_group_check=True),
                        waits=w, inc=(k == KD - 1))
                if scale_ap is None:
                    tok = V(lambda e, hf=hf: e.tensor_copy(
                        out_row[:, TH * hf:TH * (hf + 1)], pb[7][0:1, :]),
                        waits=[last])
                else:
                    tok = V(lambda e, hf=hf: e.tensor_scalar(
                        out_row[:, TH * hf:TH * (hf + 1)], pb[7][0:1, :],
                        scale_ap, None, OP.mult), waits=[last])
                bclose(7, tok)
                toks.append(tok)
            return toks, last

        def row_over_tokens(out_row, lhs_col_bf, pre_waits):
            toks = []
            for tq in range(4):
                bi = 6 if tq % 2 == 0 else 7
                w0 = bopen(bi)
                last = None
                for k in range(KD):
                    w = list(pre_waits) + w0 if (tq == 0 and k == 0) else \
                        (w0 if k == 0 else [])
                    last = T(lambda e, tq=tq, k=k, bi=bi: e.matmul(
                        pb[bi][0:1, :], lhs_col_bf[:, k:k + 1],
                        xbf[:, k, TH * tq:TH * (tq + 1)],
                        start=(k == 0), stop=(k == KD - 1), skip_group_check=True),
                        waits=w, inc=(k == KD - 1))
                tok = V(lambda e, tq=tq, bi=bi: e.tensor_copy(
                    out_row[:, TH * tq:TH * (tq + 1)], pb[bi][0:1, :]), waits=[last])
                bclose(bi, tok)
                toks.append(tok)
            return toks

        def bcast_row(dst, src_bf_row, n, pre_waits):
            toks = []
            for q in range(n // TH):
                bi = 6 if q % 2 == 0 else 7
                w0 = bopen(bi)
                mmt = T(lambda e, q=q, bi=bi: e.matmul(
                    pb[bi][:, :], onersb[:, :], src_bf_row[:, TH * q:TH * (q + 1)],
                    start=True, stop=True, skip_group_check=True),
                    waits=list(pre_waits) + w0 if q == 0 else w0, inc=True)
                tok = V(lambda e, q=q, bi=bi: e.tensor_copy(
                    dst[:, TH * q:TH * (q + 1)], pb[bi][:, :]), waits=[mmt])
                bclose(bi, tok)
                toks.append(tok)
            return toks

        def row_to_col_bf(row_f32, col_bf_dst, pre_waits):
            tcast = V(lambda e: e.tensor_copy(rowbf[:, 0:D], row_f32), waits=pre_waits)
            td1 = dma_sp(drbf[:, :], rowbf[:, 0:D], waits=[tcast])
            td2 = dma_sp(m8bf[:, :], drbf.ap().rearrange("o (k p) -> (o k) p", p=P),
                         waits=[td1])
            w0 = bopen(6)
            ttr = T(lambda e: e.matmul(
                pb[6].ap().bitcast(BF16)[:, 0:KD], m8bf[:, :], idbsb[0:KD, 0:KD],
                is_transpose=True, skip_group_check=True),
                waits=[td2, t_idbf] + w0, inc=True)
            tok = V(lambda e: e.tensor_copy(col_bf_dst,
                                            pb[6].ap().bitcast(BF16)[:, 0:KD]),
                    waits=[ttr])
            bclose(6, tok)
            return tok

        tok_eps = V(lambda e: e.memset(tiny[:, 10:11], float(EPS)))
        w0 = bopen(6)
        t1 = T(lambda e: e.matmul(pb[6][:, 0:KD], m8a[0:KD, :], idsb[0:KD, 0:KD],
                                  is_transpose=True, skip_group_check=True),
               waits=[t_wcv, t_id32] + w0, inc=True)
        tok_wcv = V(lambda e: e.tensor_copy(cbsl(C_WCV), pb[6][:, 0:KD]), waits=[t1])
        bclose(6, tok_wcv)

        w0 = bopen(7)
        tok_u = None
        for k in range(KD):
            for cc in range(KD):
                w = ([t_wq, tok_wcv] + w0) if (k == 0 and cc == 0) else []
                tok_u = T(lambda e, k=k, cc=cc: e.matmul(
                    pb[7][:, k:k + 1], qkv_slot[:, cc, P * k:P * (k + 1)],
                    cbsl(C_WCV, cc),
                    start=(cc == 0), stop=(cc == KD - 1), skip_group_check=True),
                    waits=w, inc=(k == KD - 1 and cc == KD - 1))
        tok_ubf = V(lambda e: e.tensor_scalar(cbsl(C_U), pb[7][:, 0:KD], SCALE, None,
                                              OP.mult), waits=[tok_u])
        bclose(7, tok_ubf)
        if debug:
            vv = V(lambda e: e.tensor_scalar(csl(C_U), cbsl(C_U), 1.0, None, OP.mult),
                   waits=[tok_ubf])
            dma_out(dbg["dbg_u"][:, :], csl(C_U), waits=[vv])

        t_wk = dma_g(qkv_slot, wkTd.ap().rearrange("(k p) d -> p k d", p=P),
                     waits=[tok_u], sem=dQ)

        def finish_early(tok):
            dma_out(out_fm.ap().rearrange("(k p) t -> p k t", p=P)[:, :, 0:TT],
                    xf[:, :, :], waits=[tok, t_xf[0]])
            S_.emit("P", lambda e: None, [(dO, dO.c)], None)
            S_.check()

        w0 = bopen(6)
        t_utr = T(lambda e: e.matmul(
            pb[6].ap().bitcast(BF16)[0:KD, 0:P], cbsl(C_U), idbsb[:, :],
            is_transpose=True, skip_group_check=True),
            waits=[tok_ubf, t_idbf] + w0, inc=True)
        t_ucp = V(lambda e: e.tensor_copy(m8bf[:, :],
                                          pb[6].ap().bitcast(BF16)[0:KD, 0:P]),
                  waits=[t_utr])
        bclose(6, t_ucp)
        t_ub1 = dma_g(drbf.ap().rearrange("o (k p) -> (o k) p", p=P), m8bf[:, :],
                      waits=[t_ucp], sem=dAB)
        t_ub2 = dma_g(attbc[:, 0:TT], drbf.ap()[0:1, 0:D].broadcast_to((P, D)),
                      waits=[t_ub1], sem=dAB)

        NH = 16
        NPAR = 4
        xq = xtp.ap().rearrange("(c p) d -> p c d", p=P)
        xqb = xtpb.ap().rearrange("(c p) d -> p c d", p=P)
        stt_dump = actb[:, 0, :]

        def tp_pass(bc_off, bc_tok, use_exp, st_prev, mm_prev, src=None,
                    src_rdy=None, echo=False):
            st_last = dict(st_prev)
            mm_last = dict(mm_prev)
            cast_last = {}
            echo_toks = {}
            w6 = bopen(6)
            w7 = bopen(7)
            w5 = bopen(5) if use_exp else []
            lt = None
            mm = None
            for h in range(NH):
                par = h % NPAR
                w = []
                if st_last.get(par) is not None:
                    w.append(st_last[par])
                if mm_last.get(par) is not None:
                    w.append(mm_last[par])
                if echo and h >= NPAR:
                    w.append(echo_toks[h - NPAR])
                if src_rdy is not None:
                    w.append(src_rdy[h])
                sq = xq if src is None else src
                t_d0 = dma_g(xtq[:, par, 0:2, :], sq[:, 4 * h:4 * h + 2, :],
                             waits=w, sem=dF[2 * par])
                t_d1 = dma_g(xtq[:, par, 2:4, :], sq[:, 4 * h + 2:4 * h + 4, :],
                             waits=list(w), sem=dF[2 * par + 1])
                if echo:
                    echo_toks[h] = dma_g(xqb[:, 4 * h:4 * h + 4, :],
                                         xtq[:, par, :, :],
                                         waits=[t_d0, t_d1], sem=dE[par])
                for c4 in range(4):
                    w = [t_d0] if c4 == 0 else ([t_d1] if c4 == 2 else [])
                    if c4 == 0 and h == 0:
                        w.append(bc_tok)
                    if c4 == 0 and cast_last.get(par) is not None:
                        w.append(cast_last[par])
                    lt = V(lambda e, par=par, c4=c4, bc_off=bc_off:
                           e.scalar_tensor_tensor(
                               stt_dump, xtq[:, par, c4, :], 1.0,
                               attbc[:, bc_off:bc_off + TT], OP.mult, OP.mult,
                               accum_out=acol[:, 4 * par + c4:4 * par + c4 + 1]),
                           waits=w)
                st_last[par] = lt
                if use_exp:
                    ct = SS(lambda e, par=par: e.activation(
                        bcol[:, 4 * par:4 * par + 4], acol[:, 4 * par:4 * par + 4],
                        AF.Exp), waits=[lt])
                else:
                    ct = V(lambda e, par=par: e.tensor_copy(
                        bcol[:, 4 * par:4 * par + 4], acol[:, 4 * par:4 * par + 4]),
                        waits=[lt])
                cast_last[par] = ct
                for c4 in range(4):
                    for h2 in range(2):
                        bi = 6 if h2 == 0 else 7
                        w = [ct] if (c4 == 0 and h2 == 0) else []
                        if h == 0 and c4 == 0:
                            w = w + (w6 if h2 == 0 else w7)
                        mm = T(lambda e, par=par, c4=c4, h2=h2, bi=bi, h=h:
                               e.matmul(
                                   pb[bi][0:1, 0:TH],
                                   bcol[:, 4 * par + c4:4 * par + c4 + 1],
                                   xtq[:, par, c4, TH * h2:TH * (h2 + 1)],
                                   start=(h == 0 and c4 == 0),
                                   stop=(h == NH - 1 and c4 == 3),
                                   skip_group_check=True),
                               waits=w,
                               inc=(not use_exp and c4 == 3 and h2 == 1))
                if use_exp:
                    zw = list(w5) + [t_onec] if h == 0 else []
                    mm = T(lambda e, par=par, h=h: e.matmul(
                        pb[5][0:1, 0:4], onecsb[:, :], bcol[:, 4 * par:4 * par + 4],
                        start=(h == 0), stop=(h == NH - 1), skip_group_check=True),
                        waits=zw, inc=True)
                mm_last[par] = mm
            return st_last, mm_last, lt, mm, echo_toks

        st1, mml1, tok_p1_stt, tok_p1_mm, ech1 = tp_pass(0, t_ub2, False, {}, {},
                                                         echo=True)
        t_xf.append((dma_g(xf[:, 0:4, :], xr[:, 0:4, 0:TT], sem=dX0),
                     dma_g(xf[:, 4:8, :], xr[:, 4:8, 0:TT], sem=dX1)))
        emit_w1_load((0, 0))
        emit_w2_load((0, 0))
        emit_w2_load((0, 1))
        ev6 = V(lambda e: e.tensor_copy(seg(0, TH), pb[6][0:1, 0:TH]),
                waits=[tok_p1_mm])
        ev7 = V(lambda e: e.tensor_copy(seg(TH, TH), pb[7][0:1, 0:TH]),
                waits=[tok_p1_mm])
        bclose(6, ev6)
        bclose(7, ev7)
        tok_m1c = row_to_col_bf(seg(0, D), cbsl(C_M1), [ev6, ev7])
        tok_m1g = tok_m1c
        if debug:
            vm1 = V(lambda e: e.tensor_copy(csl(C_M1), cbsl(C_M1)), waits=[tok_m1c])
            dma_out(dbg["dbg_m1"][:, :], csl(C_M1), waits=[vm1])

        r1_row = seg(2048, D)
        r1toks, tok_r1last = row_matvec(r1_row, cbsl(C_M1), qkv_slot, [tok_m1g, t_wk])
        t_wv = dma_g(qkv_slot, wvn.ap().rearrange("(k p) d -> p k d", p=P),
                     waits=[tok_r1last], sem=dQ)
        tok_mx = V(lambda e: e.reduce_max(tiny[:, 0:1], r1_row, axis=AX), waits=r1toks)
        tok_ng = V(lambda e: e.tensor_scalar(tiny[:, 1:2], tiny[:, 0:1], -1.0, None,
                                             OP.mult), waits=[tok_mx])
        tok_ex = SS(lambda e: e.activation(r1_row, r1_row, AF.Exp, bias=tiny[0:1, 1:2],
                                           scale=1.0, accum_out=tiny[:, 2:3]),
                    waits=[tok_ng])
        tok_rc = V(lambda e: e.reciprocal(tiny[:, 3:4], tiny[:, 2:3]), waits=[tok_ex])
        tok_aa = V(lambda e: e.tensor_scalar(r1_row, r1_row, tiny[0:1, 3:4], None,
                                             OP.mult), waits=[tok_rc])
        if debug:
            dma_out(dbg["dbg_aa"][:, :], r1_row, waits=[tok_aa])
        tok_aac = row_to_col_bf(r1_row, cbsl(C_AA), [tok_aa])

        w2_row = seg(2048, D)
        w2toks, tok_w2last = row_matvec(w2_row, cbsl(C_AA), qkv_slot, [tok_aac, t_wv])
        if debug:
            dma_out(dbg["dbg_w2"][:, :], w2_row, waits=w2toks)
        t_wvt = dma_g(qkv_slot, wvTd.ap().rearrange("(k p) d -> p k d", p=P),
                      waits=[tok_w2last], sem=dQ)
        t_w2cast = V(lambda e: e.tensor_copy(rowbf[:, 0:D], w2_row),
                     waits=w2toks)
        w2b_toks = bcast_row(attbc[:, TT:2 * TT], rowbf[:, 0:D], D, [t_w2cast])
        tok_w2b2 = w2b_toks[-1]

        st2, mml2, tok_p2_stt, tok_p2_mm, _ = tp_pass(TT, tok_w2b2, True, st1,
                                                      mml1, src=xqb,
                                                      src_rdy=ech1)
        ev6b = V(lambda e: e.tensor_copy(seg(0, TH), pb[6][0:1, 0:TH]),
                 waits=[tok_p2_mm])
        ev7b = V(lambda e: e.tensor_copy(seg(TH, TH), pb[7][0:1, 0:TH]),
                 waits=[tok_p2_mm])
        bclose(6, ev6b)
        bclose(7, ev7b)
        tok_z = V(lambda e: e.reduce_sum(tiny[:, 6:7], pb[5][0:1, 0:4], axis=AX),
                  waits=[tok_p2_mm])
        bclose(5, tok_z)
        tok_m2c = row_to_col_bf(seg(0, D), cbsl(C_M2), [ev6b, ev7b])

        tln = SS(lambda e: e.activation(tiny[:, 7:8], tiny[:, 6:7], AF.Ln),
                 waits=[tok_z])
        tnl = V(lambda e: e.tensor_scalar(tiny[:, 8:9], tiny[:, 7:8], -1.0, None,
                                          OP.mult), waits=[tln])
        tok_rz = SS(lambda e: e.activation(tiny[:, 9:10], tiny[:, 8:9], AF.Exp),
                    waits=[tnl])
        tok_m2g = tok_m2c

        av_row = seg(2048, D)
        avtoks, tok_avlast = row_matvec(av_row, cbsl(C_M2), qkv_slot,
                                        [tok_m2g, t_wvt, tok_rz],
                                        scale_ap=tiny[0:1, 9:10])
        t_av1 = dma_sp(drow[:, :], av_row, waits=avtoks)
        t_av2 = dma_sp(m8a[0:KD, :], drow.ap().rearrange("o (k p) -> (o k) p", p=P),
                       waits=[t_av1])
        w0 = bopen(6)
        tk = T(lambda e: e.matmul(pb[6][:, 0:KD], m8a[0:KD, :], idsb[0:KD, 0:KD],
                                  is_transpose=True, skip_group_check=True),
               waits=[t_av2] + w0, inc=True)
        tok_avc = V(lambda e: e.tensor_copy(csl(C_AV), pb[6][:, 0:KD]), waits=[tk])
        bclose(6, tok_avc)
        if debug:
            dma_out(dbg["dbg_av"][:, :], csl(C_AV), waits=[tok_avc])

        if globals().get('DBG2', False):
            dma_out(dbg["d2_tiny"][:, :], tiny[:, :], waits=[tok_rz])
            dma_out(dbg["d2_m2"][:, :], csl(C_M2), waits=[tok_m2g])
            dma_out(dbg["d2_m1"][:, :], csl(C_M1), waits=[tok_m2g])
            dma_out(dbg["d2_av"][:, :], csl(C_AV), waits=[tok_avc])
        w1_free[1] = tok_avlast
        emit_w1_load((0, 1))

        slot_tok = {0: None, 1: None}
        su_tok = {0: None, 1: None}
        x1bf_free = None
        wfm_free = None
        wrow_free = None
        wbc_free = {0: None, 1: None}
        act_free = None

        def layer_norm(pre_tok, want_x1bf, extra_first=(), attbc_free=()):
            pre_tok_per_k = pre_tok
            """stats + apply on xf (contents final per pre_tok_per_k). a|b broadcast
            goes through attbc.  Returns per-chunk apply tokens."""
            w6 = bopen(6)
            w7 = bopen(7)
            last_mm = {}
            for k in range(KD):
                for th in range(2):
                    sl = (2 * k + th) % 2
                    pk = pre_tok_per_k[k]
                    w = [pk[th] if isinstance(pk[0], tuple) else pk]
                    if k == 0 and th == 0:
                        w += list(extra_first)
                    if 2 * k + th < 2:
                        if slot_tok[sl] is not None:
                            w.append(slot_tok[sl])
                    else:
                        w.append(last_mm[2 * k + th - 2])
                    tc = V(lambda e, k=k, th=th, sl=sl: e.tensor_copy(
                        rbf[:, sl, :], xf[:, k, TH * th:TH * (th + 1)]), waits=w)
                    tq = V(lambda e, sl=sl: e.tensor_tensor(
                        sqs[:, sl, :], rbf[:, sl, :], rbf[:, sl, :], OP.mult),
                        waits=[tc])
                    bi = 6 if th == 0 else 7
                    wm = [tc, t_onec]
                    if k == 0:
                        wm += w6 if th == 0 else w7
                    T(lambda e, sl=sl, bi=bi, k=k: e.matmul(
                        pb[bi][0:1, 0:TH], onecsb[:, :], rbf[:, sl, :],
                        start=(k == 0), stop=(k == KD - 1), skip_group_check=True),
                        waits=wm, inc=False)
                    lt = T(lambda e, sl=sl, bi=bi, k=k: e.matmul(
                        pb[bi][32:33, 0:TH], onecsb[:, :], sqs[:, sl, :],
                        start=(k == 0), stop=(k == KD - 1), skip_group_check=True),
                        waits=[tq], inc=True)
                    last_mm[2 * k + th] = lt
                    slot_tok[sl] = lt
            tok_stats = last_mm[2 * KD - 1]
            ev1 = V(lambda e: e.tensor_copy(seg(0, TH), pb[6][0:1, 0:TH]),
                    waits=[tok_stats])
            ev2 = V(lambda e: e.tensor_copy(seg(TH, TH), pb[7][0:1, 0:TH]),
                    waits=[tok_stats])
            ev3 = V(lambda e: e.tensor_copy(seg(1024, TH), pb[6][32:33, 0:TH]),
                    waits=[tok_stats])
            ev4 = V(lambda e: e.tensor_copy(seg(1024 + TH, TH), pb[7][32:33, 0:TH]),
                    waits=[tok_stats])
            bclose(6, ev3)
            bclose(7, ev4)
            f1 = V(lambda e: e.tensor_scalar(seg(0, TT), seg(0, TT), 1.0 / D, None,
                                             OP.mult), waits=[ev1, ev2])
            f2 = V(lambda e: e.tensor_scalar(seg(1024, TT), seg(1024, TT), 1.0 / D,
                                             None, OP.mult), waits=[ev3, ev4])
            f3 = V(lambda e: e.tensor_tensor(seg(2048, TT), seg(0, TT), seg(0, TT),
                                             OP.mult), waits=[f1])
            f4 = V(lambda e: e.tensor_tensor(seg(1024, TT), seg(1024, TT),
                                             seg(2048, TT), OP.subtract),
                   waits=[f2, f3])
            f5 = SS(lambda e: e.activation(seg(1024, TT), seg(1024, TT), AF.Sqrt,
                                           bias=tiny[0:1, 10:11], scale=1.0),
                    waits=[f4, tok_eps])
            f6 = V(lambda e: e.reciprocal(seg(2048, TT), seg(1024, TT)), waits=[f5])
            f7 = V(lambda e: e.tensor_tensor(seg(3072, TT), seg(0, TT), seg(2048, TT),
                                             OP.mult), waits=[f6])
            f8 = V(lambda e: e.tensor_scalar(seg(3072, TT), seg(3072, TT), -1.0, None,
                                             OP.mult), waits=[f7])
            fab = V(lambda e: e.tensor_copy(rowbf[:, 0:2 * TT], seg(2048, 2 * TT)),
                    waits=[f6, f8])
            ab_toks = bcast_row(attbc[:, :], rowbf[:, 0:2 * TT], 2 * TT,
                                [fab] + list(attbc_free))
            ap_toks = []
            for k in range(KD):
                t1_ = V(lambda e, k=k: e.tensor_tensor(
                    xf[:, k, :], xf[:, k, :], attbc[:, 0:TT], OP.mult),
                    waits=[ab_toks[1]] if k == 0 else [])
                t2_ = V(lambda e, k=k: e.tensor_tensor(
                    xf[:, k, :], xf[:, k, :], attbc[:, TT:2 * TT], OP.add),
                    waits=[t1_])
                if want_x1bf:
                    w = [t2_]
                    if k == 0 and x1bf_free is not None:
                        w.append(x1bf_free)
                    t3_ = V(lambda e, k=k: e.tensor_copy(x1bf[:, k, :], xf[:, k, :]),
                            waits=w)
                    ap_toks.append(t3_)
                else:
                    ap_toks.append(t2_)
            return ap_toks

        if att_only:
            tok_dump = V(lambda e: e.tensor_copy(xf[:, 0, 0:KD], csl(C_AV)),
                         waits=[tok_avc, t_xf[0]])
            dma_out(out_fm.ap().rearrange("(k p) t -> p k t", p=P)[:, :, 0:TT],
                    xf[:, :, :], waits=[tok_dump])
            S_.emit("P", lambda e: None, [(dO, dO.c)], None)
            S_.check()
            ctx2 = True
        out_dma_tok = None
        attbc_prev = [tok_p2_stt]
        for tt in range(NTT if not att_only else 0):
            if tt > 0:
                prev_l = [None, None]
                lds = []
                for k in range(KD):
                    w = [out_k_toks[k]]
                    if prev_l[k % 2] is not None:
                        w.append(prev_l[k % 2])
                    tkk = (dma_sp if k % 2 == 0 else dma_sp2)(
                        xf[:, k, :], xr[:, k, TT * tt:TT * (tt + 1)], waits=w)
                    prev_l[k % 2] = tkk
                    lds.append(tkk)
                t_xf.append(tuple(lds))
            rpre_tok = {}
            for k in range(KD):
                rpre_tok[k] = V(lambda e, k=k: e.tensor_scalar(
                    xf[:, k, :], xf[:, k, :], csl(C_AV, k), None, OP.add),
                    waits=[t_xf[tt][k] if tt > 0 else t_xf[0][k // 4], tok_avc])
            ap_tok = layer_norm(rpre_tok, True, attbc_free=attbc_prev)
            x1_all = ap_tok[-1]
            if debug:
                nm = "dbg_x1" if tt == 0 else "dbg_x1b"
                for k in range(KD):
                    dma_out(dbg[nm][:, k, :], xf[:, k, :], waits=[x1_all])

            gl_ev = []
            gl_last = None
            for th in range(2):
                bi = 6 if th == 0 else 7
                w0b = bopen(bi)
                for k in range(KD):
                    w = list(w0b) if k == 0 else []
                    if k == 0:
                        w += [t_gw]
                    w += [ap_tok[k]]
                    gl_last = T(lambda e, th=th, k=k, bi=bi: e.matmul(
                        pb[bi][0:E, 0:TH], gwsb[:, k, :],
                        xf[:, k, TH * th:TH * (th + 1)],
                        start=(k == 0), stop=(k == KD - 1), skip_group_check=True),
                        waits=w, inc=(k == KD - 1))
                ev = V(lambda e, th=th, bi=bi: e.tensor_copy(
                    glsb[:, TH * th:TH * (th + 1)], pb[bi][0:E, 0:TH]),
                    waits=[gl_last])
                bclose(bi, ev)
                gl_ev.append(ev)
            gate_T_done = gl_last
            tok_gev = []
            for half in range(2):
                bi = 6 if half == 0 else 7
                w0b = bopen(bi)
                lt = None
                for c4 in range(4):
                    c = half * 4 + c4
                    lt = T(lambda e, c=c, c4=c4, bi=bi: e.matmul(
                        pb[bi][0:P, E * c4:E * (c4 + 1)],
                        glsb[:, P * c:P * (c + 1)], idsb[0:E, 0:E],
                        is_transpose=True,
                        start=(c4 == 0), stop=(c4 == 3), skip_group_check=True),
                        waits=(gl_ev + w0b) if c4 == 0 else [], inc=(c4 == 3))
                ev = V(lambda e, half=half, bi=bi: e.tensor_copy(
                    gsc[:, 4 * half:4 * (half + 1), :].rearrange("p a b -> p (a b)"),
                    pb[bi][:, 0:4 * E]), waits=[lt])
                bclose(bi, ev)
                tok_gev.append(ev)
            g3 = gsc[:, :, :]
            wd3 = wdtm[:, :, :]
            mx = mxt[:, :, :]
            if debug and tt == 0:
                tdl = V(lambda e: e.tensor_copy(wd3, g3), waits=tok_gev)
                dma_out(dbg["dbg_gl"][:, :, :], wd3, waits=[tdl])
            V(lambda e: e.reduce_max(mx, g3, axis=AX), waits=tok_gev)
            tsub = V(lambda e: e.tensor_tensor(g3, g3, mx.broadcast_to((P, KD, E)),
                                               OP.subtract))
            tex = SS(lambda e: e.activation(wd3.rearrange("p a b -> p (a b)"),
                                            g3.rearrange("p a b -> p (a b)"),
                                            AF.Exp), waits=[tsub])
            tsum = V(lambda e: e.reduce_sum(mx, wd3, axis=AX), waits=[tex])
            tln = SS(lambda e: e.activation(mxt2[:, :, :], mx, AF.Ln), waits=[tsum])
            tsub2 = V(lambda e: e.tensor_tensor(
                g3, g3, mxt2[:, :, 0:1].broadcast_to((P, KD, E)), OP.subtract),
                waits=[tln])
            tex2 = SS(lambda e: e.activation(g3.rearrange("p a b -> p (a b)"),
                                             g3.rearrange("p a b -> p (a b)"),
                                             AF.Exp), waits=[tsub2])
            if debug and tt == 0:
                tdp = V(lambda e: e.tensor_copy(wd3, g3), waits=[tex2])
                dma_out(dbg["dbg_gp"][:, :, :], wd3, waits=[tdp])
            V(lambda e: e.reduce_max(mx, g3, axis=AX), waits=[tex2])
            V(lambda e: e.tensor_tensor(wd3, g3, mx.broadcast_to((P, KD, E)),
                                        OP.is_equal))
            V(lambda e: e.tensor_tensor(wd3, g3, wd3, OP.subtract))
            V(lambda e: e.reduce_max(mx, wd3, axis=AX))
            V(lambda e: e.tensor_tensor(wd3, g3, mx.broadcast_to((P, KD, E)),
                                        OP.is_ge))
            tok_wd = V(lambda e: e.tensor_tensor(wd3, g3, wd3, OP.mult))
            if debug:
                dma_out(dbg["dbg_wd" if tt == 0 else "dbg_wdb"][:, :, :], wd3,
                        waits=[tok_wd])
            wfm_toks = []
            for half in range(2):
                bi = 6 if half == 0 else 7
                w0b = bopen(bi)
                lt = None
                for c4 in range(4):
                    c = half * 4 + c4
                    w = list(w0b) if c4 == 0 else []
                    if c4 == 0:
                        w.append(tok_wd)
                        if wfm_free is not None:
                            w.append(wfm_free)
                    lt = T(lambda e, c=c, c4=c4, bi=bi: e.matmul(
                        pb[bi][0:E, P * c4:P * (c4 + 1)], wdtm[:, c, :],
                        idsb[:, :], is_transpose=True,
                        start=(c4 == 0), stop=(c4 == 3), skip_group_check=True),
                        waits=w, inc=(c4 == 3))
                ev = V(lambda e, half=half, bi=bi: e.tensor_copy(
                    wfm[:, TH * half:TH * (half + 1)], pb[bi][0:E, 0:TH]),
                    waits=[lt])
                bclose(bi, ev)
                wfm_toks.append(ev)

            first_eo_add = True
            last_add_kth = {}
            for e_i in range(NE):
                has_w = e_i < E
                key = (tt, e_i)
                par1 = w1_par[key]
                par2 = w2_par[key]
                pos = tt * NE + e_i
                nxt = seq[pos + 1] if pos + 1 < len(seq) else None
                if has_w:
                    ea_ = e_i % 2
                    wv_ = list(wfm_toks)
                    if wbc_free[ea_] is not None:
                        wv_.append(wbc_free[ea_])
                    dma_s(dwbd[ea_].ap()[0:1, :], wfm[e_i:e_i + 1, :], dWB[ea_],
                          waits=wv_)
                    tok_wb = dma_s(
                        wbc[:, ea_, :],
                        dwbd[ea_].ap()[0:1, :].broadcast_to((P, TT)), dWB[ea_])
                last_h = None
                act_last = {0: None, 1: None}
                for u_i, (j, th) in enumerate([(j, th) for j in range(JH)
                                               for th in range(2)]):
                    yb = u_i % 2
                    gb = 2 + u_i % 2
                    wy = bopen(yb)
                    wg = bopen(gb)
                    if u_i == 0:
                        wy = wy + [t_w1[key], x1_all]
                    lty = ltg = None
                    for k in range(KD):
                        lty = T(lambda e, j=j, th=th, k=k, yb=yb, par1=par1: e.matmul(
                            pb[yb][:, :], w1b[:, par1, k, P * j:P * (j + 1)],
                            x1bf[:, k, TH * th:TH * (th + 1)],
                            start=(k == 0), stop=(k == KD - 1),
                            skip_group_check=True),
                            waits=wy if k == 0 else [], inc=(k == KD - 1))
                    for k in range(KD):
                        ltg = T(lambda e, j=j, th=th, k=k, gb=gb, par1=par1: e.matmul(
                            pb[gb][:, :],
                            w1b[:, par1, k, P * (JH + j):P * (JH + j + 1)],
                            x1bf[:, k, TH * th:TH * (th + 1)],
                            start=(k == 0), stop=(k == KD - 1),
                            skip_group_check=True),
                            waits=wg if k == 0 else [], inc=(k == KD - 1))
                    last_h = ltg
                    ssl = u_i % 2
                    wsil = [ltg]
                    if su_tok[ssl] is not None:
                        wsil.append(su_tok[ssl])
                    tsil = SS(lambda e, ssl=ssl, gb=gb: e.activation(
                        silu_s[:, ssl, :], pb[gb][:, :], AF.Silu), waits=wsil)
                    if has_w:
                        tyw = V(lambda e, ssl=ssl, yb=yb, e_i=e_i, th=th:
                                e.tensor_tensor(
                                    yw_s[:, ssl, :], pb[yb][:, :],
                                    wbc[:, e_i % 2, TH * th:TH * (th + 1)], OP.mult),
                                waits=[lty] + ([tok_wb] if u_i == 0 else []))
                        wbc_free[e_i % 2] = tyw
                        wact = [tsil, tyw]
                        if u_i == 0 and act_free is not None:
                            wact.append(act_free)
                        tact = V(lambda e, ssl=ssl, j=j, th=th: e.tensor_tensor(
                            actb[:, j, TH * th:TH * (th + 1)],
                            yw_s[:, ssl, :], silu_s[:, ssl, :], OP.mult), waits=wact)
                        bclose(yb, tyw)
                        bclose(gb, tact)
                    else:
                        wact = [tsil, lty]
                        if u_i == 0 and act_free is not None:
                            wact.append(act_free)
                        tact = V(lambda e, ssl=ssl, j=j, th=th, yb=yb: e.tensor_tensor(
                            actb[:, j, TH * th:TH * (th + 1)],
                            pb[yb][:, :], silu_s[:, ssl, :], OP.mult), waits=wact)
                        bclose(yb, tact)
                        bclose(gb, tact)
                    su_tok[ssl] = tact
                    act_last[th] = tact
                    if debug and tt == 0 and e_i == 0 and j == 0 and th == 0:
                        vv = V(lambda e, yb=yb: e.tensor_copy(
                            dbgst[:, :], pb[yb][:, :]), waits=[lty])
                        dma_out(dbg["dbg_h0"][:, :], dbgst[:, :], waits=[vv])
                w1_free[par1] = last_h
                if e_i == NE - 1 and tt == NTT - 1:
                    pass
                if nxt is not None:
                    emit_w1_load(nxt)
                last_b = None
                for k in range(KD):
                    for th in range(2):
                        eb = 4 + (2 * k + th) % 2
                        w0b = bopen(eb)
                        lt = None
                        for j in range(JH):
                            w = list(w0b) if j == 0 else []
                            if j == 0:
                                w.append(act_last[th])
                                if k == 0 and th == 0:
                                    w.append(t_w2[key])
                            lt = T(lambda e, k=k, th=th, j=j, eb=eb, par2=par2:
                                   e.matmul(
                                       pb[eb][:, :],
                                       w2b[:, par2, j, P * k:P * (k + 1)],
                                       actb[:, j, TH * th:TH * (th + 1)],
                                       start=(j == 0), stop=(j == JH - 1),
                                       skip_group_check=True),
                                   waits=w, inc=(j == JH - 1))
                        wadd = [lt]
                        if first_eo_add:
                            wadd += [gate_T_done, ap_tok[k]]
                        tadd = V(lambda e, k=k, th=th, eb=eb: e.tensor_tensor(
                            xf[:, k, TH * th:TH * (th + 1)],
                            xf[:, k, TH * th:TH * (th + 1)], pb[eb][:, :], OP.add),
                            waits=wadd)
                        bclose(eb, tadd)
                        last_b = lt
                        last_b_add = tadd
                        if e_i == NE - 1:
                            last_add_kth[(k, th)] = tadd
                first_eo_add = False
                act_free = last_b
                w2_free[par2] = last_b
                if nxt is not None:
                    emit_w2_load(nxt)
                if e_i == NE - 1:
                    x1bf_free = last_h

            r2_tok = {k: (last_add_kth[(k, 0)], last_add_kth[(k, 1)])
                      for k in range(KD)}
            if debug:
                nm2 = "dbg_r2" if tt == 0 else "dbg_r2b"
                for k in range(KD):
                    dma_out(dbg[nm2][:, k, :], xf[:, k, :], waits=[last_b_add])
            out_toks = layer_norm(r2_tok, False,
                                  attbc_free=[x1_all, tok_gev[-1]])
            attbc_prev = [out_toks[-1]]
            ofm = out_fm.ap().rearrange("(k p) t -> p k t", p=P)
            prev_o = [None, None]
            out_k_toks = []
            for k in range(KD):
                w = [out_toks[k]]
                if prev_o[k % 2] is not None:
                    w.append(prev_o[k % 2])
                tkk = (dma_out if k % 2 == 0 else dma_out2)(
                    ofm[:, k, TT * tt:TT * (tt + 1)], xf[:, k, :], waits=w)
                prev_o[k % 2] = tkk
                out_k_toks.append(tkk)

        if not att_only:
            S_.emit("P", lambda e: None, [(dO, dO.c), (dO2, dO2.c)], None)
            S_.check()

        def replay(eng_obj, key):
            for fn, waits, inc in S_.ops[key]:
                seen = {}
                for s, v in waits:
                    if s.name in seen:
                        seen[s.name] = (s, max(v, seen[s.name][1]))
                    else:
                        seen[s.name] = (s, v)
                for s, v in seen.values():
                    eng_obj.wait_ge(s.h, v)
                ins = fn(eng_obj)
                if inc is not None:
                    assert ins is not None
                    ins.then_inc(inc[0].h, inc[1])

        with nc.Block() as block:
            @block.tensor
            def _(t):
                replay(t, "T")

            @block.vector
            def _(v):
                replay(v, "V")

            @block.scalar
            def _(s):
                replay(s, "S")

            @block.gpsimd
            def _(g):
                replay(g, "G")

            @block.sync
            def _(sp):
                replay(sp, "P")

    return nc


_NC_CACHE = {}


def _get_nc(debug=False):
    if debug not in _NC_CACHE:
        _NC_CACHE[debug] = build_program(debug)
    return _NC_CACHE[debug]


def _prep_inputs(inputs):
    x = np.ascontiguousarray(np.asarray(inputs["L0_attn_x"], dtype=np.float32))
    w_qkv = np.ascontiguousarray(np.asarray(inputs["L0_attn_w_qkv"], dtype=np.float32))
    w_conv1 = np.ascontiguousarray(
        np.asarray(inputs["L0_attn_w_conv1"], dtype=np.float32))
    gate_w = np.asarray(inputs["L0_ffn_gate_w"], dtype=np.float32)
    W1 = np.asarray(inputs["L0_ffn_W1"], dtype=np.float32)
    W2 = np.asarray(inputs["L0_ffn_W2"], dtype=np.float32)
    Ws1 = np.asarray(inputs["L0_ffn_Ws1"], dtype=np.float32)
    Ws2 = np.asarray(inputs["L0_ffn_Ws2"], dtype=np.float32)

    Ws1T = Ws1.T
    Ws2T = Ws2.T
    shared = {
        "w1t": np.ascontiguousarray(W1.transpose(0, 2, 1)),
        "w2t": np.ascontiguousarray(W2.transpose(0, 2, 1)),
        "ws1a": np.ascontiguousarray(
            np.concatenate([Ws1T[:, 0:H], Ws1T[:, 2 * H:3 * H]], axis=1)),
        "ws1b": np.ascontiguousarray(
            np.concatenate([Ws1T[:, H:2 * H], Ws1T[:, 3 * H:4 * H]], axis=1)),
        "ws2a": np.ascontiguousarray(Ws2T[0:H]),
        "ws2b": np.ascontiguousarray(Ws2T[H:2 * H]),
        "wqn": np.ascontiguousarray(w_qkv[0:D]),
        "wvn": np.ascontiguousarray(w_qkv[2 * D:3 * D]),
        "wkTd": np.ascontiguousarray(w_qkv[D:2 * D].T),
        "wvTd": np.ascontiguousarray(w_qkv[2 * D:3 * D].T),
        "wconv": np.ascontiguousarray(w_conv1.reshape(1, D)),
        "gwT": np.ascontiguousarray(gate_w.T),
        "id32": np.eye(P, dtype=np.float32),
        "idbf": np.eye(P, dtype=ml_dtypes.bfloat16),
        "onec": np.ones((P, 1), dtype=ml_dtypes.bfloat16),
        "oner": np.ones((1, P), dtype=ml_dtypes.bfloat16),
        "oner32": np.ones((1, P), dtype=np.float32),
    }
    xT_b = [np.ascontiguousarray(x[b].T) for b in range(2)]
    xfull_b = [np.ascontiguousarray(x[b]) for b in range(2)]
    in_maps = []
    for c in range(8):
        b, s0 = c // 4, (c % 4) * TPC
        m = dict(shared)
        m["xT"] = np.ascontiguousarray(xT_b[b][:, s0:s0 + TPC])
        m["xtp"] = xfull_b[b]
        in_maps.append(m)
    return in_maps


def kernel(**inputs):
    in_maps = _prep_inputs(inputs)
    nc = _get_nc(False)
    res = run_bass_kernel_spmd(nc, in_maps, core_ids=list(range(8)))
    out = np.empty((2, 8192, D), dtype=np.float32)
    for c in range(8):
        b, s0 = c // 4, (c % 4) * TPC
        out[b, s0:s0 + TPC] = res.results[c]["out_fm"].T
    return out


# version 10
# speedup vs baseline: 1.0415x; 1.0415x over previous
import contextlib
import numpy as np
import ml_dtypes
import concourse.bass as bass
import concourse.mybir as mybir
from concourse.bass_utils import run_bass_kernel_spmd

F32 = mybir.dt.float32
BF16 = mybir.dt.bfloat16
AX = mybir.AxisListType.X
OP = mybir.AluOpType
AF = mybir.ActivationFunctionType

P = 128
D = 1024
KD = 8
TPC = 2048
NTT = 2
TT = 1024
TH = 512
E = 8
NE = 10
H = 768
JH = 6
SCALE = 1.0 / 32.0
EPS = 1e-5

ENGS = ("T", "V", "S", "G", "P")


class Sem:
    def __init__(self, handle, name):
        self.h = handle
        self.name = name
        self.c = 0


class Script:
    def __init__(self):
        self.ops = {e: [] for e in ENGS}

    def emit(self, eng, fn, waits=(), inc=None):
        waits = tuple((s, v) for t_ in waits if t_ is not None
                      for (s, v) in [t_] if s is not None)
        tok = None
        if inc is not None:
            sem, n = inc
            sem.c += n
            tok = (sem, sem.c)
            inc = (sem, n)
        self.ops[eng].append((fn, waits, inc))
        return tok

    def check(self):
        ptr = {e: 0 for e in ENGS}
        val = {}
        progress = True
        while progress:
            progress = False
            for e in ENGS:
                while ptr[e] < len(self.ops[e]):
                    fn, waits, inc = self.ops[e][ptr[e]]
                    if any(val.get(s.name, 0) < v for s, v in waits):
                        break
                    if inc is not None:
                        val[inc[0].name] = val.get(inc[0].name, 0) + inc[1]
                    ptr[e] += 1
                    progress = True
        stuck = [e for e in ENGS if ptr[e] < len(self.ops[e])]
        if stuck:
            det = []
            for e in stuck:
                fn, waits, inc = self.ops[e][ptr[e]]
                blk = [(s.name, v, val.get(s.name, 0)) for s, v in waits
                       if val.get(s.name, 0) < v]
                det.append(f"{e}[{ptr[e]}/{len(self.ops[e])}] blocked on {blk}")
            raise RuntimeError("DEADLOCK: " + "; ".join(det))


def build_program(debug=False, att_only=False, att_level=None):
    nc = bass.Bass("TRN2", target_bir_lowering=False)

    xT = nc.dram_tensor("xT", [D, TPC], F32, kind="ExternalInput")
    w1t = nc.dram_tensor("w1t", [E, D, 2 * H], F32, kind="ExternalInput")
    w2t = nc.dram_tensor("w2t", [E, H, D], F32, kind="ExternalInput")
    ws1a = nc.dram_tensor("ws1a", [D, 2 * H], F32, kind="ExternalInput")
    ws1b = nc.dram_tensor("ws1b", [D, 2 * H], F32, kind="ExternalInput")
    ws2a = nc.dram_tensor("ws2a", [H, D], F32, kind="ExternalInput")
    ws2b = nc.dram_tensor("ws2b", [H, D], F32, kind="ExternalInput")
    wqn = nc.dram_tensor("wqn", [D, D], F32, kind="ExternalInput")
    wvn = nc.dram_tensor("wvn", [D, D], F32, kind="ExternalInput")
    wkTd = nc.dram_tensor("wkTd", [D, D], F32, kind="ExternalInput")
    wvTd = nc.dram_tensor("wvTd", [D, D], F32, kind="ExternalInput")
    wconv = nc.dram_tensor("wconv", [1, D], F32, kind="ExternalInput")
    gwT = nc.dram_tensor("gwT", [D, E], F32, kind="ExternalInput")
    id32 = nc.dram_tensor("id32", [P, P], F32, kind="ExternalInput")
    idbf = nc.dram_tensor("idbf", [P, P], BF16, kind="ExternalInput")
    onec = nc.dram_tensor("onec", [P, 1], BF16, kind="ExternalInput")
    oner = nc.dram_tensor("oner", [1, P], BF16, kind="ExternalInput")
    out_fm = nc.dram_tensor("out_fm", [D, TPC], F32, kind="ExternalOutput")
    xtp = nc.dram_tensor("xtp", [4 * TPC, D], F32, kind="ExternalInput")
    oner32 = nc.dram_tensor("oner32", [1, P], F32, kind="ExternalInput")
    xtpb = nc.dram_tensor("xtpb", [4 * TPC, D], BF16)
    drbf = nc.dram_tensor("drbf", [1, D], BF16)
    drow = nc.dram_tensor("drow", [1, D], F32)
    dag = nc.dram_tensor("dag", [1, TPC], BF16)
    dfb = [nc.dram_tensor("dfb0", [1, TH], BF16),
           nc.dram_tensor("dfb1", [1, TH], BF16)]
    dwbd = [nc.dram_tensor("dwb0", [1, TT], BF16),
            nc.dram_tensor("dwb1", [1, TT], BF16)]

    dbg = {}
    if globals().get('DBG2', False):
        dbg["d2_tiny"] = nc.dram_tensor("d2_tiny", [1, 16], F32, kind="ExternalOutput")
        dbg["d2_m2"] = nc.dram_tensor("d2_m2", [P, KD], F32, kind="ExternalOutput")
        dbg["d2_m1"] = nc.dram_tensor("d2_m1", [P, KD], F32, kind="ExternalOutput")
        dbg["d2_av"] = nc.dram_tensor("d2_av", [P, KD], F32, kind="ExternalOutput")
    if debug:
        for name, shape in [
            ("dbg_u", [P, KD]), ("dbg_ag", [1, TPC]), ("dbg_m1", [P, KD]),
            ("dbg_aa", [1, D]), ("dbg_w2", [1, D]), ("dbg_s", [1, TPC]),
            ("dbg_av", [P, KD]), ("dbg_x1", [P, KD, TT]),
            ("dbg_wd", [P, KD, E]), ("dbg_gl", [P, KD, E]), ("dbg_gp", [P, KD, E]),
            ("dbg_r2", [P, KD, TT]),
            ("dbg_x1b", [P, KD, TT]), ("dbg_r2b", [P, KD, TT]),
            ("dbg_wdb", [P, KD, E]),
        ]:
            dbg[name] = nc.dram_tensor(name, shape, F32, kind="ExternalOutput")
        dbg["dbg_h0"] = nc.dram_tensor("dbg_h0", [P, TH], BF16, kind="ExternalOutput") \
            if debug else None

    ctx = contextlib.ExitStack()
    with ctx:
        sbt = lambda name, shape, dt: ctx.enter_context(nc.sbuf_tensor(name, shape, dt))
        pst = lambda name: ctx.enter_context(nc.psum_tensor(name, [P, TH], F32))
        mks = lambda name: Sem(ctx.enter_context(nc.semaphore(name)), name)

        xtq = sbt("xtq", [P, 4, 4, TT], BF16)
        acol = sbt("acol", [P, 16], F32)
        bcol = sbt("bcol", [P, 16], BF16)
        xf = sbt("xf", [P, KD, TT], F32)
        x1bf = sbt("x1bf", [P, KD, TT], BF16)
        w1b = sbt("w1b", [P, 2, KD, 2 * H], BF16)
        w2b = sbt("w2b", [P, 2, JH, D], BF16)
        actb = sbt("actb", [P, JH, TT], BF16)
        wbc = sbt("wbc", [P, 2, TT], BF16)
        attbc = sbt("attbc", [P, TPC], BF16)
        silu_s = sbt("silu_s", [P, 2, TH], BF16)
        yw_s = sbt("yw_s", [P, 2, TH], BF16)
        rbf = sbt("rbf", [P, 2, TH], BF16)
        sqs = sbt("sqs", [P, 2, TH], BF16)
        gsc = sbt("gsc", [P, KD, E], F32)
        wdtm = sbt("wdtm", [P, KD, E], F32)
        mxt = sbt("mxt", [P, KD, 1], F32)
        mxt2 = sbt("mxt2", [P, KD, 1], F32)
        wfm = sbt("wfm", [E, TT], BF16)
        row32 = sbt("row32", [1, 4096], F32)
        rowbf = sbt("rowbf", [1, TPC], BF16)
        m8a = sbt("m8a", [KD + 1, P], F32)
        m8bf = sbt("m8bf", [KD, P], BF16)
        cols = sbt("cols", [P, 9 * KD], F32)
        colbf = sbt("colbf", [P, 8 * KD], BF16)
        tiny = sbt("tiny", [1, 16], F32)
        idsb = sbt("idsb", [P, P], F32)
        idbsb = sbt("idbsb", [P, P], BF16)
        onecsb = sbt("onecsb", [P, 1], BF16)
        onersb = sbt("onersb", [1, P], BF16)
        onersb32 = sbt("onersb32", [1, P], F32)
        gwsb = sbt("gwsb", [P, KD, E], F32)
        dbgst = sbt("dbgst", [P, TH], BF16) if debug else None

        C_U, C_M1, C_M2, C_AA, C_W2, C_AV, C_WCV, C_TMP, C_ED = \
            0, 1, 2, 3, 4, 5, 6, 7, 8
        csl = lambda w, f=None: cols[:, w * KD:(w + 1) * KD] if f is None \
            else cols[:, w * KD + f: w * KD + f + 1]
        cbsl = lambda w, f=None: colbf[:, w * KD:(w + 1) * KD] if f is None \
            else colbf[:, w * KD + f: w * KD + f + 1]

        seg = lambda a, n: row32[:, a:a + n]
        glsb = attbc.ap().bitcast(F32)[0:E, :]

        pb = [pst(f"pb{i}") for i in range(8)]

        sT, sV, sS = mks("sT"), mks("sV"), mks("sS")
        dP, dG, dO = mks("dP"), mks("dG"), mks("dO")
        dP2, dO2 = mks("dP2"), mks("dO2")
        dX0, dX1 = mks("dX0"), mks("dX1")
        dE = [mks(f"dE{i}") for i in range(4)]
        dXB, dQ = mks("dXB"), mks("dQ")
        dAB = mks("dAB")
        dFB = [mks("dFB0"), mks("dFB1")]
        dWB = [mks("dWB0"), mks("dWB1")]
        dF = [mks(f"dF{i}") for i in range(8)]
        dW1 = [mks("dW10"), mks("dW11")]
        dW2 = [mks("dW20"), mks("dW21")]
        S_ = Script()

        def T(fn, waits=(), inc=False):
            return S_.emit("T", fn, waits, (sT, 1) if inc else None)

        def V(fn, waits=(), inc=True):
            return S_.emit("V", fn, waits, (sV, 1) if inc else None)

        def SS(fn, waits=(), inc=True):
            return S_.emit("S", fn, waits, (sS, 1) if inc else None)

        def dma_sp(dst, src, waits=()):
            return S_.emit("P", lambda e, d=dst, s=src: e.dma_start(out=d, in_=s),
                           waits, (dP, 16))

        def dma_out(dst, src, waits=()):
            return S_.emit("P", lambda e, d=dst, s=src: e.dma_start(out=d, in_=s),
                           waits, (dO, 16))

        def dma_sp2(dst, src, waits=()):
            return S_.emit("P", lambda e, d=dst, s=src: e.dma_start(out=d, in_=s),
                           waits, (dP2, 16))

        def dma_out2(dst, src, waits=()):
            return S_.emit("P", lambda e, d=dst, s=src: e.dma_start(out=d, in_=s),
                           waits, (dO2, 16))

        def dma_s(dst, src, sem, waits=()):
            return S_.emit("S", lambda e, d=dst, s=src: e.dma_start(out=d, in_=s),
                           waits, (sem, 16))

        def bcast_dma(dst_2d, src_row, dram_t, sem, waits=()):
            n = src_row.shape[-1]
            t1_ = dma_s(dram_t.ap()[0:1, 0:n], src_row, sem, waits=waits)
            return dma_s(dst_2d, dram_t.ap()[0:1, 0:n].broadcast_to((P, n)),
                         sem, waits=[t1_])

        def dma_g(dst, src, waits=(), sem=None):
            return S_.emit("G", lambda e, d=dst, s=src: e.dma_start(out=d, in_=s),
                           waits, (sem if sem is not None else dG, 16))

        bank_tok = [None] * 8

        def bopen(i):
            w = bank_tok[i]
            bank_tok[i] = None
            return [w] if w is not None else []

        def bclose(i, tok):
            bank_tok[i] = tok

        t_id32 = dma_sp(idsb[:, :], id32[:, :])
        t_idbf = dma_sp(idbsb[:, :], idbf[:, :])
        t_onec = dma_sp(onecsb[:, :], onec[:, :])
        t_oner = dma_sp(onersb[:, :], oner[:, :])
        t_oner32 = dma_sp(onersb32[:, :], oner32[:, :])
        t_gw = dma_sp(gwsb[:, :, :], gwT.ap().rearrange("(k p) e -> p k e", p=P))
        t_wcv = dma_sp(m8a[0:KD, :], wconv.ap().rearrange("o (k p) -> (o k) p", p=P))
        xr = xT.ap().rearrange("(k p) t -> p k t", p=P)
        t_xf = []
        qkv_slot = w1b[:, 1, :, 0:D]
        t_wq = dma_g(qkv_slot, wqn.ap().rearrange("(k p) d -> p k d", p=P), sem=dQ)

        seq = [(tt, i) for tt in range(NTT) for i in range(NE)]
        w1_par = {}
        w2_par = {}
        for n, key in enumerate(seq):
            w1_par[key] = n % 2
            w2_par[key] = n % 2

        def w1_src(i):
            if i < E:
                return w1t.ap()[i].rearrange("(k p) h -> p k h", p=P)
            return (ws1a if i == E else ws1b).ap().rearrange("(k p) h -> p k h", p=P)

        def w2_src(i):
            if i < E:
                return w2t.ap()[i].rearrange("(j p) d -> p j d", p=P)
            return (ws2a if i == E else ws2b).ap().rearrange("(j p) d -> p j d", p=P)

        t_w1 = {}
        t_w2 = {}
        w1_free = {0: None, 1: None}
        w2_free = {0: None, 1: None}

        def emit_w1_load(key):
            if key in t_w1:
                return
            par = w1_par[key]
            t_w1[key] = dma_g(w1b[:, par, :, :], w1_src(key[1]),
                              waits=[w1_free[par]] if w1_free[par] else [],
                              sem=dW1[par])

        def emit_w2_load(key):
            if key in t_w2:
                return
            par = w2_par[key]
            t_w2[key] = dma_g(w2b[:, par, :, :], w2_src(key[1]),
                              waits=[w2_free[par]] if w2_free[par] else [],
                              sem=dW2[par])


        def row_matvec(out_row, lhs_col_bf, rhs3, pre_waits, scale_ap=None):
            toks = []
            last = None
            for hf in range(2):
                w0 = bopen(7)
                for k in range(KD):
                    w = list(pre_waits) + w0 if (hf == 0 and k == 0) else \
                        (w0 if k == 0 else [])
                    last = T(lambda e, hf=hf, k=k: e.matmul(
                        pb[7][0:1, :], lhs_col_bf[:, k:k + 1],
                        rhs3[:, k, TH * hf:TH * (hf + 1)],
                        start=(k == 0), stop=(k == KD - 1), skip_group_check=True),
                        waits=w, inc=(k == KD - 1))
                if scale_ap is None:
                    tok = V(lambda e, hf=hf: e.tensor_copy(
                        out_row[:, TH * hf:TH * (hf + 1)], pb[7][0:1, :]),
                        waits=[last])
                else:
                    tok = V(lambda e, hf=hf: e.tensor_scalar(
                        out_row[:, TH * hf:TH * (hf + 1)], pb[7][0:1, :],
                        scale_ap, None, OP.mult), waits=[last])
                bclose(7, tok)
                toks.append(tok)
            return toks, last

        def row_over_tokens(out_row, lhs_col_bf, pre_waits):
            toks = []
            for tq in range(4):
                bi = 6 if tq % 2 == 0 else 7
                w0 = bopen(bi)
                last = None
                for k in range(KD):
                    w = list(pre_waits) + w0 if (tq == 0 and k == 0) else \
                        (w0 if k == 0 else [])
                    last = T(lambda e, tq=tq, k=k, bi=bi: e.matmul(
                        pb[bi][0:1, :], lhs_col_bf[:, k:k + 1],
                        xbf[:, k, TH * tq:TH * (tq + 1)],
                        start=(k == 0), stop=(k == KD - 1), skip_group_check=True),
                        waits=w, inc=(k == KD - 1))
                tok = V(lambda e, tq=tq, bi=bi: e.tensor_copy(
                    out_row[:, TH * tq:TH * (tq + 1)], pb[bi][0:1, :]), waits=[last])
                bclose(bi, tok)
                toks.append(tok)
            return toks

        def bcast_row(dst, src_bf_row, n, pre_waits):
            toks = []
            for q in range(n // TH):
                bi = 6 if q % 2 == 0 else 7
                w0 = bopen(bi)
                mmt = T(lambda e, q=q, bi=bi: e.matmul(
                    pb[bi][:, :], onersb[:, :], src_bf_row[:, TH * q:TH * (q + 1)],
                    start=True, stop=True, skip_group_check=True),
                    waits=list(pre_waits) + w0 if q == 0 else w0, inc=True)
                tok = V(lambda e, q=q, bi=bi: e.tensor_copy(
                    dst[:, TH * q:TH * (q + 1)], pb[bi][:, :]), waits=[mmt])
                bclose(bi, tok)
                toks.append(tok)
            return toks

        def row_to_col_bf(row_f32, col_bf_dst, pre_waits):
            tcast = V(lambda e: e.tensor_copy(rowbf[:, 0:D], row_f32), waits=pre_waits)
            td1 = dma_sp(drbf[:, :], rowbf[:, 0:D], waits=[tcast])
            td2 = dma_sp(m8bf[:, :], drbf.ap().rearrange("o (k p) -> (o k) p", p=P),
                         waits=[td1])
            w0 = bopen(6)
            ttr = T(lambda e: e.matmul(
                pb[6].ap().bitcast(BF16)[:, 0:KD], m8bf[:, :], idbsb[0:KD, 0:KD],
                is_transpose=True, skip_group_check=True),
                waits=[td2, t_idbf] + w0, inc=True)
            tok = V(lambda e: e.tensor_copy(col_bf_dst,
                                            pb[6].ap().bitcast(BF16)[:, 0:KD]),
                    waits=[ttr])
            bclose(6, tok)
            return tok

        tok_eps = V(lambda e: e.memset(tiny[:, 10:11], float(EPS)))
        w0 = bopen(6)
        t1 = T(lambda e: e.matmul(pb[6][:, 0:KD], m8a[0:KD, :], idsb[0:KD, 0:KD],
                                  is_transpose=True, skip_group_check=True),
               waits=[t_wcv, t_id32] + w0, inc=True)
        tok_wcv = V(lambda e: e.tensor_copy(cbsl(C_WCV), pb[6][:, 0:KD]), waits=[t1])
        bclose(6, tok_wcv)

        w0 = bopen(7)
        tok_u = None
        for k in range(KD):
            for cc in range(KD):
                w = ([t_wq, tok_wcv] + w0) if (k == 0 and cc == 0) else []
                tok_u = T(lambda e, k=k, cc=cc: e.matmul(
                    pb[7][:, k:k + 1], qkv_slot[:, cc, P * k:P * (k + 1)],
                    cbsl(C_WCV, cc),
                    start=(cc == 0), stop=(cc == KD - 1), skip_group_check=True),
                    waits=w, inc=(k == KD - 1 and cc == KD - 1))
        tok_ubf = V(lambda e: e.tensor_scalar(cbsl(C_U), pb[7][:, 0:KD], SCALE, None,
                                              OP.mult), waits=[tok_u])
        bclose(7, tok_ubf)
        if debug:
            vv = V(lambda e: e.tensor_scalar(csl(C_U), cbsl(C_U), 1.0, None, OP.mult),
                   waits=[tok_ubf])
            dma_out(dbg["dbg_u"][:, :], csl(C_U), waits=[vv])

        t_wk = dma_g(qkv_slot, wkTd.ap().rearrange("(k p) d -> p k d", p=P),
                     waits=[tok_u], sem=dQ)

        def finish_early(tok):
            dma_out(out_fm.ap().rearrange("(k p) t -> p k t", p=P)[:, :, 0:TT],
                    xf[:, :, :], waits=[tok, t_xf[0]])
            S_.emit("P", lambda e: None, [(dO, dO.c)], None)
            S_.check()

        w0 = bopen(6)
        t_utr = T(lambda e: e.matmul(
            pb[6].ap().bitcast(BF16)[0:KD, 0:P], cbsl(C_U), idbsb[:, :],
            is_transpose=True, skip_group_check=True),
            waits=[tok_ubf, t_idbf] + w0, inc=True)
        t_ucp = V(lambda e: e.tensor_copy(m8bf[:, :],
                                          pb[6].ap().bitcast(BF16)[0:KD, 0:P]),
                  waits=[t_utr])
        bclose(6, t_ucp)
        t_ub1 = dma_g(drbf.ap().rearrange("o (k p) -> (o k) p", p=P), m8bf[:, :],
                      waits=[t_ucp], sem=dAB)
        t_ub2 = dma_g(attbc[:, 0:TT], drbf.ap()[0:1, 0:D].broadcast_to((P, D)),
                      waits=[t_ub1], sem=dAB)

        NH = 16
        NPAR = 4
        xq = xtp.ap().rearrange("(c p) d -> p c d", p=P)
        xqb = xtpb.ap().rearrange("(c p) d -> p c d", p=P)
        stt_dump = actb[:, 0, :]

        def tp_pass(bc_off, bc_tok, use_exp, st_prev, mm_prev, src=None,
                    src_rdy=None, echo=False):
            st_last = dict(st_prev)
            mm_last = dict(mm_prev)
            cast_last = {}
            echo_toks = {}
            w6 = bopen(6)
            w7 = bopen(7)
            w5 = bopen(5) if use_exp else []
            lt = None
            mm = None
            for h in range(NH):
                par = h % NPAR
                w = []
                if st_last.get(par) is not None:
                    w.append(st_last[par])
                if mm_last.get(par) is not None:
                    w.append(mm_last[par])
                if echo and h >= NPAR:
                    w.append(echo_toks[h - NPAR])
                if src_rdy is not None:
                    w.append(src_rdy[h])
                sq = xq if src is None else src
                t_d0 = dma_g(xtq[:, par, 0:2, :], sq[:, 4 * h:4 * h + 2, :],
                             waits=w, sem=dF[2 * par])
                t_d1 = dma_g(xtq[:, par, 2:4, :], sq[:, 4 * h + 2:4 * h + 4, :],
                             waits=list(w), sem=dF[2 * par + 1])
                if echo:
                    echo_toks[h] = dma_g(xqb[:, 4 * h:4 * h + 4, :],
                                         xtq[:, par, :, :],
                                         waits=[t_d0, t_d1], sem=dE[par])
                for c4 in range(4):
                    w = [t_d0] if c4 == 0 else ([t_d1] if c4 == 2 else [])
                    if c4 == 0 and h == 0:
                        w.append(bc_tok)
                    if c4 == 0 and cast_last.get(par) is not None:
                        w.append(cast_last[par])
                    lt = V(lambda e, par=par, c4=c4, bc_off=bc_off:
                           e.scalar_tensor_tensor(
                               stt_dump, xtq[:, par, c4, :], 1.0,
                               attbc[:, bc_off:bc_off + TT], OP.mult, OP.mult,
                               accum_out=acol[:, 4 * par + c4:4 * par + c4 + 1]),
                           waits=w)
                st_last[par] = lt
                if use_exp:
                    ct = SS(lambda e, par=par: e.activation(
                        bcol[:, 4 * par:4 * par + 4], acol[:, 4 * par:4 * par + 4],
                        AF.Exp), waits=[lt])
                else:
                    ct = V(lambda e, par=par: e.tensor_copy(
                        bcol[:, 4 * par:4 * par + 4], acol[:, 4 * par:4 * par + 4]),
                        waits=[lt])
                cast_last[par] = ct
                for c4 in range(4):
                    for h2 in range(2):
                        bi = 6 if h2 == 0 else 7
                        w = [ct] if (c4 == 0 and h2 == 0) else []
                        if h == 0 and c4 == 0:
                            w = w + (w6 if h2 == 0 else w7)
                        mm = T(lambda e, par=par, c4=c4, h2=h2, bi=bi, h=h:
                               e.matmul(
                                   pb[bi][0:1, 0:TH],
                                   bcol[:, 4 * par + c4:4 * par + c4 + 1],
                                   xtq[:, par, c4, TH * h2:TH * (h2 + 1)],
                                   start=(h == 0 and c4 == 0),
                                   stop=(h == NH - 1 and c4 == 3),
                                   skip_group_check=True),
                               waits=w,
                               inc=(not use_exp and c4 == 3 and h2 == 1))
                if use_exp:
                    zw = list(w5) + [t_onec] if h == 0 else []
                    mm = T(lambda e, par=par, h=h: e.matmul(
                        pb[5][0:1, 0:4], onecsb[:, :], bcol[:, 4 * par:4 * par + 4],
                        start=(h == 0), stop=(h == NH - 1), skip_group_check=True),
                        waits=zw, inc=True)
                mm_last[par] = mm
            return st_last, mm_last, lt, mm, echo_toks

        st1, mml1, tok_p1_stt, tok_p1_mm, ech1 = tp_pass(0, t_ub2, False,
                                                         {}, {})
        t_xf.append((dma_g(xf[:, 0:4, :], xr[:, 0:4, 0:TT], sem=dX0),
                     dma_g(xf[:, 4:8, :], xr[:, 4:8, 0:TT], sem=dX1)))
        emit_w1_load((0, 0))
        emit_w2_load((0, 0))
        emit_w2_load((0, 1))
        ev6 = V(lambda e: e.tensor_copy(seg(0, TH), pb[6][0:1, 0:TH]),
                waits=[tok_p1_mm])
        ev7 = V(lambda e: e.tensor_copy(seg(TH, TH), pb[7][0:1, 0:TH]),
                waits=[tok_p1_mm])
        bclose(6, ev6)
        bclose(7, ev7)
        tok_m1c = row_to_col_bf(seg(0, D), cbsl(C_M1), [ev6, ev7])
        tok_m1g = tok_m1c
        if debug:
            vm1 = V(lambda e: e.tensor_copy(csl(C_M1), cbsl(C_M1)), waits=[tok_m1c])
            dma_out(dbg["dbg_m1"][:, :], csl(C_M1), waits=[vm1])

        r1_row = seg(2048, D)
        r1toks, tok_r1last = row_matvec(r1_row, cbsl(C_M1), qkv_slot, [tok_m1g, t_wk])
        t_wv = dma_g(qkv_slot, wvn.ap().rearrange("(k p) d -> p k d", p=P),
                     waits=[tok_r1last], sem=dQ)
        tok_mx = V(lambda e: e.reduce_max(tiny[:, 0:1], r1_row, axis=AX), waits=r1toks)
        tok_ng = V(lambda e: e.tensor_scalar(tiny[:, 1:2], tiny[:, 0:1], -1.0, None,
                                             OP.mult), waits=[tok_mx])
        tok_ex = SS(lambda e: e.activation(r1_row, r1_row, AF.Exp, bias=tiny[0:1, 1:2],
                                           scale=1.0, accum_out=tiny[:, 2:3]),
                    waits=[tok_ng])
        tok_rc = V(lambda e: e.reciprocal(tiny[:, 3:4], tiny[:, 2:3]), waits=[tok_ex])
        tok_aa = V(lambda e: e.tensor_scalar(r1_row, r1_row, tiny[0:1, 3:4], None,
                                             OP.mult), waits=[tok_rc])
        if debug:
            dma_out(dbg["dbg_aa"][:, :], r1_row, waits=[tok_aa])
        tok_aac = row_to_col_bf(r1_row, cbsl(C_AA), [tok_aa])

        w2_row = seg(2048, D)
        w2toks, tok_w2last = row_matvec(w2_row, cbsl(C_AA), qkv_slot, [tok_aac, t_wv])
        if debug:
            dma_out(dbg["dbg_w2"][:, :], w2_row, waits=w2toks)
        t_wvt = dma_g(qkv_slot, wvTd.ap().rearrange("(k p) d -> p k d", p=P),
                      waits=[tok_w2last], sem=dQ)
        t_w2cast = V(lambda e: e.tensor_copy(rowbf[:, 0:D], w2_row),
                     waits=w2toks)
        w2b_toks = bcast_row(attbc[:, TT:2 * TT], rowbf[:, 0:D], D, [t_w2cast])
        tok_w2b2 = w2b_toks[-1]

        st2, mml2, tok_p2_stt, tok_p2_mm, _ = tp_pass(TT, tok_w2b2, True,
                                                      st1, mml1)
        ev6b = V(lambda e: e.tensor_copy(seg(0, TH), pb[6][0:1, 0:TH]),
                 waits=[tok_p2_mm])
        ev7b = V(lambda e: e.tensor_copy(seg(TH, TH), pb[7][0:1, 0:TH]),
                 waits=[tok_p2_mm])
        bclose(6, ev6b)
        bclose(7, ev7b)
        tok_z = V(lambda e: e.reduce_sum(tiny[:, 6:7], pb[5][0:1, 0:4], axis=AX),
                  waits=[tok_p2_mm])
        bclose(5, tok_z)
        tok_m2c = row_to_col_bf(seg(0, D), cbsl(C_M2), [ev6b, ev7b])

        tln = SS(lambda e: e.activation(tiny[:, 7:8], tiny[:, 6:7], AF.Ln),
                 waits=[tok_z])
        tnl = V(lambda e: e.tensor_scalar(tiny[:, 8:9], tiny[:, 7:8], -1.0, None,
                                          OP.mult), waits=[tln])
        tok_rz = SS(lambda e: e.activation(tiny[:, 9:10], tiny[:, 8:9], AF.Exp),
                    waits=[tnl])
        tok_m2g = tok_m2c

        av_row = seg(2048, D)
        avtoks, tok_avlast = row_matvec(av_row, cbsl(C_M2), qkv_slot,
                                        [tok_m2g, t_wvt, tok_rz],
                                        scale_ap=tiny[0:1, 9:10])
        t_av1 = dma_sp(drow[:, :], av_row, waits=avtoks)
        t_av2 = dma_sp(m8a[0:KD, :], drow.ap().rearrange("o (k p) -> (o k) p", p=P),
                       waits=[t_av1])
        w0 = bopen(6)
        tk = T(lambda e: e.matmul(pb[6][:, 0:KD], m8a[0:KD, :], idsb[0:KD, 0:KD],
                                  is_transpose=True, skip_group_check=True),
               waits=[t_av2] + w0, inc=True)
        tok_avc = V(lambda e: e.tensor_copy(csl(C_AV), pb[6][:, 0:KD]), waits=[tk])
        bclose(6, tok_avc)
        if debug:
            dma_out(dbg["dbg_av"][:, :], csl(C_AV), waits=[tok_avc])

        if globals().get('DBG2', False):
            dma_out(dbg["d2_tiny"][:, :], tiny[:, :], waits=[tok_rz])
            dma_out(dbg["d2_m2"][:, :], csl(C_M2), waits=[tok_m2g])
            dma_out(dbg["d2_m1"][:, :], csl(C_M1), waits=[tok_m2g])
            dma_out(dbg["d2_av"][:, :], csl(C_AV), waits=[tok_avc])
        w1_free[1] = tok_avlast
        emit_w1_load((0, 1))

        slot_tok = {0: None, 1: None}
        su_tok = {0: None, 1: None}
        x1bf_free = None
        wfm_free = None
        wrow_free = None
        wbc_free = {0: None, 1: None}
        act_free = None

        def layer_norm(pre_tok, want_x1bf, extra_first=(), attbc_free=()):
            pre_tok_per_k = pre_tok
            """stats + apply on xf (contents final per pre_tok_per_k). a|b broadcast
            goes through attbc.  Returns per-chunk apply tokens."""
            w6 = bopen(6)
            w7 = bopen(7)
            last_mm = {}
            for k in range(KD):
                for th in range(2):
                    sl = (2 * k + th) % 2
                    pk = pre_tok_per_k[k]
                    w = [pk[th] if isinstance(pk[0], tuple) else pk]
                    if k == 0 and th == 0:
                        w += list(extra_first)
                    if 2 * k + th < 2:
                        if slot_tok[sl] is not None:
                            w.append(slot_tok[sl])
                    else:
                        w.append(last_mm[2 * k + th - 2])
                    tc = V(lambda e, k=k, th=th, sl=sl: e.tensor_copy(
                        rbf[:, sl, :], xf[:, k, TH * th:TH * (th + 1)]), waits=w)
                    tq = V(lambda e, sl=sl: e.tensor_tensor(
                        sqs[:, sl, :], rbf[:, sl, :], rbf[:, sl, :], OP.mult),
                        waits=[tc])
                    bi = 6 if th == 0 else 7
                    wm = [tc, t_onec]
                    if k == 0:
                        wm += w6 if th == 0 else w7
                    T(lambda e, sl=sl, bi=bi, k=k: e.matmul(
                        pb[bi][0:1, 0:TH], onecsb[:, :], rbf[:, sl, :],
                        start=(k == 0), stop=(k == KD - 1), skip_group_check=True),
                        waits=wm, inc=False)
                    lt = T(lambda e, sl=sl, bi=bi, k=k: e.matmul(
                        pb[bi][32:33, 0:TH], onecsb[:, :], sqs[:, sl, :],
                        start=(k == 0), stop=(k == KD - 1), skip_group_check=True),
                        waits=[tq], inc=True)
                    last_mm[2 * k + th] = lt
                    slot_tok[sl] = lt
            tok_stats = last_mm[2 * KD - 1]
            ev1 = V(lambda e: e.tensor_copy(seg(0, TH), pb[6][0:1, 0:TH]),
                    waits=[tok_stats])
            ev2 = V(lambda e: e.tensor_copy(seg(TH, TH), pb[7][0:1, 0:TH]),
                    waits=[tok_stats])
            ev3 = V(lambda e: e.tensor_copy(seg(1024, TH), pb[6][32:33, 0:TH]),
                    waits=[tok_stats])
            ev4 = V(lambda e: e.tensor_copy(seg(1024 + TH, TH), pb[7][32:33, 0:TH]),
                    waits=[tok_stats])
            bclose(6, ev3)
            bclose(7, ev4)
            f3 = V(lambda e: e.scalar_tensor_tensor(
                seg(2048, TT), seg(0, TT), 1.0 / D, seg(0, TT),
                OP.mult, OP.mult), waits=[ev1, ev2])
            f4 = V(lambda e: e.tensor_tensor(seg(1024, TT), seg(1024, TT),
                                             seg(2048, TT), OP.subtract),
                   waits=[ev3, ev4, f3])
            f5 = SS(lambda e: e.activation(seg(1024, TT), seg(1024, TT), AF.Sqrt,
                                           bias=tiny[0:1, 10:11], scale=1.0 / D),
                    waits=[f4, tok_eps])
            f6 = V(lambda e: e.reciprocal(seg(2048, TT), seg(1024, TT)), waits=[f5])
            f7 = V(lambda e: e.scalar_tensor_tensor(
                seg(3072, TT), seg(0, TT), -1.0 / D, seg(2048, TT),
                OP.mult, OP.mult), waits=[f6])
            fab = V(lambda e: e.tensor_copy(rowbf[:, 0:2 * TT], seg(2048, 2 * TT)),
                    waits=[f6, f7])
            ab_toks = bcast_row(attbc[:, :], rowbf[:, 0:2 * TT], 2 * TT,
                                [fab] + list(attbc_free))
            ap_toks = []
            for k in range(KD):
                t1_ = V(lambda e, k=k: e.tensor_tensor(
                    xf[:, k, :], xf[:, k, :], attbc[:, 0:TT], OP.mult),
                    waits=[ab_toks[1]] if k == 0 else [])
                t2_ = V(lambda e, k=k: e.tensor_tensor(
                    xf[:, k, :], xf[:, k, :], attbc[:, TT:2 * TT], OP.add),
                    waits=[t1_])
                if want_x1bf:
                    w = [t2_]
                    if k == 0 and x1bf_free is not None:
                        w.append(x1bf_free)
                    t3_ = V(lambda e, k=k: e.tensor_copy(x1bf[:, k, :], xf[:, k, :]),
                            waits=w)
                    ap_toks.append(t3_)
                else:
                    ap_toks.append(t2_)
            return ap_toks

        if att_only:
            tok_dump = V(lambda e: e.tensor_copy(xf[:, 0, 0:KD], csl(C_AV)),
                         waits=[tok_avc, t_xf[0]])
            dma_out(out_fm.ap().rearrange("(k p) t -> p k t", p=P)[:, :, 0:TT],
                    xf[:, :, :], waits=[tok_dump])
            S_.emit("P", lambda e: None, [(dO, dO.c)], None)
            S_.check()
            ctx2 = True
        out_dma_tok = None
        attbc_prev = [tok_p2_stt]
        for tt in range(NTT if not att_only else 0):
            if tt > 0:
                prev_l = [None, None]
                lds = []
                for k in range(KD):
                    w = [out_k_toks[k]]
                    if prev_l[k % 2] is not None:
                        w.append(prev_l[k % 2])
                    tkk = (dma_sp if k % 2 == 0 else dma_sp2)(
                        xf[:, k, :], xr[:, k, TT * tt:TT * (tt + 1)], waits=w)
                    prev_l[k % 2] = tkk
                    lds.append(tkk)
                t_xf.append(tuple(lds))
            rpre_tok = {}
            for k in range(KD):
                rpre_tok[k] = V(lambda e, k=k: e.tensor_scalar(
                    xf[:, k, :], xf[:, k, :], csl(C_AV, k), None, OP.add),
                    waits=[t_xf[tt][k] if tt > 0 else t_xf[0][k // 4], tok_avc])
            ap_tok = layer_norm(rpre_tok, True, attbc_free=attbc_prev)
            x1_all = ap_tok[-1]
            if debug:
                nm = "dbg_x1" if tt == 0 else "dbg_x1b"
                for k in range(KD):
                    dma_out(dbg[nm][:, k, :], xf[:, k, :], waits=[x1_all])

            gl_ev = []
            gl_last = None
            for th in range(2):
                bi = 6 if th == 0 else 7
                w0b = bopen(bi)
                for k in range(KD):
                    w = list(w0b) if k == 0 else []
                    if k == 0:
                        w += [t_gw]
                    w += [ap_tok[k]]
                    gl_last = T(lambda e, th=th, k=k, bi=bi: e.matmul(
                        pb[bi][0:E, 0:TH], gwsb[:, k, :],
                        xf[:, k, TH * th:TH * (th + 1)],
                        start=(k == 0), stop=(k == KD - 1), skip_group_check=True),
                        waits=w, inc=(k == KD - 1))
                ev = V(lambda e, th=th, bi=bi: e.tensor_copy(
                    glsb[:, TH * th:TH * (th + 1)], pb[bi][0:E, 0:TH]),
                    waits=[gl_last])
                bclose(bi, ev)
                gl_ev.append(ev)
            gate_T_done = gl_last
            tok_gev = []
            for half in range(2):
                bi = 6 if half == 0 else 7
                w0b = bopen(bi)
                lt = None
                for c4 in range(4):
                    c = half * 4 + c4
                    lt = T(lambda e, c=c, c4=c4, bi=bi: e.matmul(
                        pb[bi][0:P, E * c4:E * (c4 + 1)],
                        glsb[:, P * c:P * (c + 1)], idsb[0:E, 0:E],
                        is_transpose=True,
                        start=(c4 == 0), stop=(c4 == 3), skip_group_check=True),
                        waits=(gl_ev + w0b) if c4 == 0 else [], inc=(c4 == 3))
                ev = V(lambda e, half=half, bi=bi: e.tensor_copy(
                    gsc[:, 4 * half:4 * (half + 1), :].rearrange("p a b -> p (a b)"),
                    pb[bi][:, 0:4 * E]), waits=[lt])
                bclose(bi, ev)
                tok_gev.append(ev)
            g3 = gsc[:, :, :]
            wd3 = wdtm[:, :, :]
            mx = mxt[:, :, :]
            if debug and tt == 0:
                tdl = V(lambda e: e.tensor_copy(wd3, g3), waits=tok_gev)
                dma_out(dbg["dbg_gl"][:, :, :], wd3, waits=[tdl])
            V(lambda e: e.reduce_max(mx, g3, axis=AX), waits=tok_gev)
            tsub = V(lambda e: e.tensor_tensor(g3, g3, mx.broadcast_to((P, KD, E)),
                                               OP.subtract))
            tex = SS(lambda e: e.activation(wd3.rearrange("p a b -> p (a b)"),
                                            g3.rearrange("p a b -> p (a b)"),
                                            AF.Exp), waits=[tsub])
            tsum = V(lambda e: e.reduce_sum(mx, wd3, axis=AX), waits=[tex])
            tln = SS(lambda e: e.activation(mxt2[:, :, :], mx, AF.Ln), waits=[tsum])
            tsub2 = V(lambda e: e.tensor_tensor(
                g3, g3, mxt2[:, :, 0:1].broadcast_to((P, KD, E)), OP.subtract),
                waits=[tln])
            tex2 = SS(lambda e: e.activation(g3.rearrange("p a b -> p (a b)"),
                                             g3.rearrange("p a b -> p (a b)"),
                                             AF.Exp), waits=[tsub2])
            if debug and tt == 0:
                tdp = V(lambda e: e.tensor_copy(wd3, g3), waits=[tex2])
                dma_out(dbg["dbg_gp"][:, :, :], wd3, waits=[tdp])
            V(lambda e: e.reduce_max(mx, g3, axis=AX), waits=[tex2])
            V(lambda e: e.tensor_tensor(wd3, g3, mx.broadcast_to((P, KD, E)),
                                        OP.is_equal))
            V(lambda e: e.tensor_tensor(wd3, g3, wd3, OP.subtract))
            V(lambda e: e.reduce_max(mx, wd3, axis=AX))
            V(lambda e: e.tensor_tensor(wd3, g3, mx.broadcast_to((P, KD, E)),
                                        OP.is_ge))
            tok_wd = V(lambda e: e.tensor_tensor(wd3, g3, wd3, OP.mult))
            if debug:
                dma_out(dbg["dbg_wd" if tt == 0 else "dbg_wdb"][:, :, :], wd3,
                        waits=[tok_wd])
            wfm_toks = []
            for half in range(2):
                bi = 6 if half == 0 else 7
                w0b = bopen(bi)
                lt = None
                for c4 in range(4):
                    c = half * 4 + c4
                    w = list(w0b) if c4 == 0 else []
                    if c4 == 0:
                        w.append(tok_wd)
                        if wfm_free is not None:
                            w.append(wfm_free)
                    lt = T(lambda e, c=c, c4=c4, bi=bi: e.matmul(
                        pb[bi][0:E, P * c4:P * (c4 + 1)], wdtm[:, c, :],
                        idsb[:, :], is_transpose=True,
                        start=(c4 == 0), stop=(c4 == 3), skip_group_check=True),
                        waits=w, inc=(c4 == 3))
                ev = V(lambda e, half=half, bi=bi: e.tensor_copy(
                    wfm[:, TH * half:TH * (half + 1)], pb[bi][0:E, 0:TH]),
                    waits=[lt])
                bclose(bi, ev)
                wfm_toks.append(ev)

            first_eo_add = True
            last_add_kth = {}
            for e_i in range(NE):
                has_w = e_i < E
                key = (tt, e_i)
                par1 = w1_par[key]
                par2 = w2_par[key]
                pos = tt * NE + e_i
                nxt = seq[pos + 1] if pos + 1 < len(seq) else None
                if has_w:
                    ea_ = e_i % 2
                    wv_ = list(wfm_toks)
                    if wbc_free[ea_] is not None:
                        wv_.append(wbc_free[ea_])
                    dma_s(dwbd[ea_].ap()[0:1, :], wfm[e_i:e_i + 1, :], dWB[ea_],
                          waits=wv_)
                    tok_wb = dma_s(
                        wbc[:, ea_, :],
                        dwbd[ea_].ap()[0:1, :].broadcast_to((P, TT)), dWB[ea_])
                last_h = None
                act_last = {0: None, 1: None}
                for u_i, (j, th) in enumerate([(j, th) for j in range(JH)
                                               for th in range(2)]):
                    yb = u_i % 2
                    gb = 2 + u_i % 2
                    wy = bopen(yb)
                    wg = bopen(gb)
                    if u_i == 0:
                        wy = wy + [t_w1[key], x1_all]
                    lty = ltg = None
                    for k in range(KD):
                        lty = T(lambda e, j=j, th=th, k=k, yb=yb, par1=par1: e.matmul(
                            pb[yb][:, :], w1b[:, par1, k, P * j:P * (j + 1)],
                            x1bf[:, k, TH * th:TH * (th + 1)],
                            start=(k == 0), stop=(k == KD - 1),
                            skip_group_check=True),
                            waits=wy if k == 0 else [], inc=(k == KD - 1))
                    for k in range(KD):
                        ltg = T(lambda e, j=j, th=th, k=k, gb=gb, par1=par1: e.matmul(
                            pb[gb][:, :],
                            w1b[:, par1, k, P * (JH + j):P * (JH + j + 1)],
                            x1bf[:, k, TH * th:TH * (th + 1)],
                            start=(k == 0), stop=(k == KD - 1),
                            skip_group_check=True),
                            waits=wg if k == 0 else [], inc=(k == KD - 1))
                    last_h = ltg
                    ssl = u_i % 2
                    wsil = [ltg]
                    if su_tok[ssl] is not None:
                        wsil.append(su_tok[ssl])
                    tsil = SS(lambda e, ssl=ssl, gb=gb: e.activation(
                        silu_s[:, ssl, :], pb[gb][:, :], AF.Silu), waits=wsil)
                    if has_w:
                        tyw = V(lambda e, ssl=ssl, yb=yb, e_i=e_i, th=th:
                                e.tensor_tensor(
                                    yw_s[:, ssl, :], pb[yb][:, :],
                                    wbc[:, e_i % 2, TH * th:TH * (th + 1)], OP.mult),
                                waits=[lty] + ([tok_wb] if u_i == 0 else []))
                        wbc_free[e_i % 2] = tyw
                        wact = [tsil, tyw]
                        if u_i == 0 and act_free is not None:
                            wact.append(act_free)
                        tact = V(lambda e, ssl=ssl, j=j, th=th: e.tensor_tensor(
                            actb[:, j, TH * th:TH * (th + 1)],
                            yw_s[:, ssl, :], silu_s[:, ssl, :], OP.mult), waits=wact)
                        bclose(yb, tyw)
                        bclose(gb, tact)
                    else:
                        wact = [tsil, lty]
                        if u_i == 0 and act_free is not None:
                            wact.append(act_free)
                        tact = V(lambda e, ssl=ssl, j=j, th=th, yb=yb: e.tensor_tensor(
                            actb[:, j, TH * th:TH * (th + 1)],
                            pb[yb][:, :], silu_s[:, ssl, :], OP.mult), waits=wact)
                        bclose(yb, tact)
                        bclose(gb, tact)
                    su_tok[ssl] = tact
                    act_last[th] = tact
                    if debug and tt == 0 and e_i == 0 and j == 0 and th == 0:
                        vv = V(lambda e, yb=yb: e.tensor_copy(
                            dbgst[:, :], pb[yb][:, :]), waits=[lty])
                        dma_out(dbg["dbg_h0"][:, :], dbgst[:, :], waits=[vv])
                w1_free[par1] = last_h
                if e_i == NE - 1 and tt == NTT - 1:
                    pass
                if nxt is not None:
                    emit_w1_load(nxt)
                last_b = None
                for k in range(KD):
                    for th in range(2):
                        eb = 4 + (2 * k + th) % 2
                        w0b = bopen(eb)
                        lt = None
                        for j in range(JH):
                            w = list(w0b) if j == 0 else []
                            if j == 0:
                                w.append(act_last[th])
                                if k == 0 and th == 0:
                                    w.append(t_w2[key])
                            lt = T(lambda e, k=k, th=th, j=j, eb=eb, par2=par2:
                                   e.matmul(
                                       pb[eb][:, :],
                                       w2b[:, par2, j, P * k:P * (k + 1)],
                                       actb[:, j, TH * th:TH * (th + 1)],
                                       start=(j == 0), stop=(j == JH - 1),
                                       skip_group_check=True),
                                   waits=w, inc=(j == JH - 1))
                        wadd = [lt]
                        if first_eo_add:
                            wadd += [gate_T_done, ap_tok[k]]
                        tadd = V(lambda e, k=k, th=th, eb=eb: e.tensor_tensor(
                            xf[:, k, TH * th:TH * (th + 1)],
                            xf[:, k, TH * th:TH * (th + 1)], pb[eb][:, :], OP.add),
                            waits=wadd)
                        bclose(eb, tadd)
                        last_b = lt
                        last_b_add = tadd
                        if e_i == NE - 1:
                            last_add_kth[(k, th)] = tadd
                first_eo_add = False
                act_free = last_b
                w2_free[par2] = last_b
                if nxt is not None:
                    emit_w2_load(nxt)
                if e_i == NE - 1:
                    x1bf_free = last_h

            r2_tok = {k: (last_add_kth[(k, 0)], last_add_kth[(k, 1)])
                      for k in range(KD)}
            if debug:
                nm2 = "dbg_r2" if tt == 0 else "dbg_r2b"
                for k in range(KD):
                    dma_out(dbg[nm2][:, k, :], xf[:, k, :], waits=[last_b_add])
            out_toks = layer_norm(r2_tok, False,
                                  attbc_free=[x1_all, tok_gev[-1]])
            attbc_prev = [out_toks[-1]]
            ofm = out_fm.ap().rearrange("(k p) t -> p k t", p=P)
            prev_o = [None, None]
            out_k_toks = []
            for k in range(KD):
                w = [out_toks[k]]
                if prev_o[k % 2] is not None:
                    w.append(prev_o[k % 2])
                tkk = (dma_out if k % 2 == 0 else dma_out2)(
                    ofm[:, k, TT * tt:TT * (tt + 1)], xf[:, k, :], waits=w)
                prev_o[k % 2] = tkk
                out_k_toks.append(tkk)

        if not att_only:
            S_.emit("P", lambda e: None, [(dO, dO.c), (dO2, dO2.c)], None)
            S_.check()

        def replay(eng_obj, key):
            for fn, waits, inc in S_.ops[key]:
                seen = {}
                for s, v in waits:
                    if s.name in seen:
                        seen[s.name] = (s, max(v, seen[s.name][1]))
                    else:
                        seen[s.name] = (s, v)
                for s, v in seen.values():
                    eng_obj.wait_ge(s.h, v)
                ins = fn(eng_obj)
                if inc is not None:
                    assert ins is not None
                    ins.then_inc(inc[0].h, inc[1])

        with nc.Block() as block:
            @block.tensor
            def _(t):
                replay(t, "T")

            @block.vector
            def _(v):
                replay(v, "V")

            @block.scalar
            def _(s):
                replay(s, "S")

            @block.gpsimd
            def _(g):
                replay(g, "G")

            @block.sync
            def _(sp):
                replay(sp, "P")

    return nc


_NC_CACHE = {}


def _get_nc(debug=False):
    if debug not in _NC_CACHE:
        _NC_CACHE[debug] = build_program(debug)
    return _NC_CACHE[debug]


def _prep_inputs(inputs):
    x = np.ascontiguousarray(np.asarray(inputs["L0_attn_x"], dtype=np.float32))
    w_qkv = np.ascontiguousarray(np.asarray(inputs["L0_attn_w_qkv"], dtype=np.float32))
    w_conv1 = np.ascontiguousarray(
        np.asarray(inputs["L0_attn_w_conv1"], dtype=np.float32))
    gate_w = np.asarray(inputs["L0_ffn_gate_w"], dtype=np.float32)
    W1 = np.asarray(inputs["L0_ffn_W1"], dtype=np.float32)
    W2 = np.asarray(inputs["L0_ffn_W2"], dtype=np.float32)
    Ws1 = np.asarray(inputs["L0_ffn_Ws1"], dtype=np.float32)
    Ws2 = np.asarray(inputs["L0_ffn_Ws2"], dtype=np.float32)

    Ws1T = Ws1.T
    Ws2T = Ws2.T
    shared = {
        "w1t": np.ascontiguousarray(W1.transpose(0, 2, 1)),
        "w2t": np.ascontiguousarray(W2.transpose(0, 2, 1)),
        "ws1a": np.ascontiguousarray(
            np.concatenate([Ws1T[:, 0:H], Ws1T[:, 2 * H:3 * H]], axis=1)),
        "ws1b": np.ascontiguousarray(
            np.concatenate([Ws1T[:, H:2 * H], Ws1T[:, 3 * H:4 * H]], axis=1)),
        "ws2a": np.ascontiguousarray(Ws2T[0:H]),
        "ws2b": np.ascontiguousarray(Ws2T[H:2 * H]),
        "wqn": np.ascontiguousarray(w_qkv[0:D]),
        "wvn": np.ascontiguousarray(w_qkv[2 * D:3 * D]),
        "wkTd": np.ascontiguousarray(w_qkv[D:2 * D].T),
        "wvTd": np.ascontiguousarray(w_qkv[2 * D:3 * D].T),
        "wconv": np.ascontiguousarray(w_conv1.reshape(1, D)),
        "gwT": np.ascontiguousarray(gate_w.T),
        "id32": np.eye(P, dtype=np.float32),
        "idbf": np.eye(P, dtype=ml_dtypes.bfloat16),
        "onec": np.ones((P, 1), dtype=ml_dtypes.bfloat16),
        "oner": np.ones((1, P), dtype=ml_dtypes.bfloat16),
        "oner32": np.ones((1, P), dtype=np.float32),
    }
    xT_b = [np.ascontiguousarray(x[b].T) for b in range(2)]
    xfull_b = [np.ascontiguousarray(x[b]) for b in range(2)]
    in_maps = []
    for c in range(8):
        b, s0 = c // 4, (c % 4) * TPC
        m = dict(shared)
        m["xT"] = np.ascontiguousarray(xT_b[b][:, s0:s0 + TPC])
        m["xtp"] = xfull_b[b]
        in_maps.append(m)
    return in_maps


def kernel(**inputs):
    in_maps = _prep_inputs(inputs)
    nc = _get_nc(False)
    res = run_bass_kernel_spmd(nc, in_maps, core_ids=list(range(8)))
    out = np.empty((2, 8192, D), dtype=np.float32)
    for c in range(8):
        b, s0 = c // 4, (c % 4) * TPC
        out[b, s0:s0 + TPC] = res.results[c]["out_fm"].T
    return out


# version 11 (final)
# speedup vs baseline: 1.0680x; 1.0255x over previous
import contextlib
import numpy as np
import ml_dtypes
import concourse.bass as bass
import concourse.mybir as mybir
from concourse.bass_utils import run_bass_kernel_spmd

F32 = mybir.dt.float32
BF16 = mybir.dt.bfloat16
AX = mybir.AxisListType.X
OP = mybir.AluOpType
AF = mybir.ActivationFunctionType

P = 128
D = 1024
KD = 8
TPC = 2048
NTT = 2
TT = 1024
TH = 512
E = 8
NE = 10
H = 768
JH = 6
SCALE = 1.0 / 32.0
EPS = 1e-5

ENGS = ("T", "V", "S", "G", "P")


class Sem:
    def __init__(self, handle, name):
        self.h = handle
        self.name = name
        self.c = 0


class Script:
    def __init__(self):
        self.ops = {e: [] for e in ENGS}

    def emit(self, eng, fn, waits=(), inc=None):
        waits = tuple((s, v) for t_ in waits if t_ is not None
                      for (s, v) in [t_] if s is not None)
        tok = None
        if inc is not None:
            sem, n = inc
            sem.c += n
            tok = (sem, sem.c)
            inc = (sem, n)
        self.ops[eng].append((fn, waits, inc))
        return tok

    def check(self):
        ptr = {e: 0 for e in ENGS}
        val = {}
        progress = True
        while progress:
            progress = False
            for e in ENGS:
                while ptr[e] < len(self.ops[e]):
                    fn, waits, inc = self.ops[e][ptr[e]]
                    if any(val.get(s.name, 0) < v for s, v in waits):
                        break
                    if inc is not None:
                        val[inc[0].name] = val.get(inc[0].name, 0) + inc[1]
                    ptr[e] += 1
                    progress = True
        stuck = [e for e in ENGS if ptr[e] < len(self.ops[e])]
        if stuck:
            det = []
            for e in stuck:
                fn, waits, inc = self.ops[e][ptr[e]]
                blk = [(s.name, v, val.get(s.name, 0)) for s, v in waits
                       if val.get(s.name, 0) < v]
                det.append(f"{e}[{ptr[e]}/{len(self.ops[e])}] blocked on {blk}")
            raise RuntimeError("DEADLOCK: " + "; ".join(det))


def build_program(debug=False, att_only=False, att_level=None):
    nc = bass.Bass("TRN2", target_bir_lowering=False)

    xT = nc.dram_tensor("xT", [D, TPC], F32, kind="ExternalInput")
    w1t = nc.dram_tensor("w1t", [E, D, 2 * H], F32, kind="ExternalInput")
    w2t = nc.dram_tensor("w2t", [E, H, D], F32, kind="ExternalInput")
    ws1a = nc.dram_tensor("ws1a", [D, 2 * H], F32, kind="ExternalInput")
    ws1b = nc.dram_tensor("ws1b", [D, 2 * H], F32, kind="ExternalInput")
    ws2a = nc.dram_tensor("ws2a", [H, D], F32, kind="ExternalInput")
    ws2b = nc.dram_tensor("ws2b", [H, D], F32, kind="ExternalInput")
    wqn = nc.dram_tensor("wqn", [D, D], F32, kind="ExternalInput")
    wvn = nc.dram_tensor("wvn", [D, D], F32, kind="ExternalInput")
    wkTd = nc.dram_tensor("wkTd", [D, D], F32, kind="ExternalInput")
    wvTd = nc.dram_tensor("wvTd", [D, D], F32, kind="ExternalInput")
    wconv = nc.dram_tensor("wconv", [1, D], F32, kind="ExternalInput")
    gwT = nc.dram_tensor("gwT", [D, E], F32, kind="ExternalInput")
    id32 = nc.dram_tensor("id32", [P, P], F32, kind="ExternalInput")
    idbf = nc.dram_tensor("idbf", [P, P], BF16, kind="ExternalInput")
    onec = nc.dram_tensor("onec", [P, 1], BF16, kind="ExternalInput")
    oner = nc.dram_tensor("oner", [1, P], BF16, kind="ExternalInput")
    out_fm = nc.dram_tensor("out_fm", [D, TPC], F32, kind="ExternalOutput")
    xtp = nc.dram_tensor("xtp", [4 * TPC, D], F32, kind="ExternalInput")
    oner32 = nc.dram_tensor("oner32", [1, P], F32, kind="ExternalInput")
    xtpb = nc.dram_tensor("xtpb", [4 * TPC, D], BF16)
    drbf = nc.dram_tensor("drbf", [1, D], BF16)
    drow = nc.dram_tensor("drow", [1, D], F32)
    dag = nc.dram_tensor("dag", [1, TPC], BF16)
    dfb = [nc.dram_tensor("dfb0", [1, TH], BF16),
           nc.dram_tensor("dfb1", [1, TH], BF16)]
    dwbd = [nc.dram_tensor("dwb0", [1, TT], BF16),
            nc.dram_tensor("dwb1", [1, TT], BF16)]

    dbg = {}
    if globals().get('DBG2', False):
        dbg["d2_tiny"] = nc.dram_tensor("d2_tiny", [1, 16], F32, kind="ExternalOutput")
        dbg["d2_m2"] = nc.dram_tensor("d2_m2", [P, KD], F32, kind="ExternalOutput")
        dbg["d2_m1"] = nc.dram_tensor("d2_m1", [P, KD], F32, kind="ExternalOutput")
        dbg["d2_av"] = nc.dram_tensor("d2_av", [P, KD], F32, kind="ExternalOutput")
    if debug:
        for name, shape in [
            ("dbg_u", [P, KD]), ("dbg_ag", [1, TPC]), ("dbg_m1", [P, KD]),
            ("dbg_aa", [1, D]), ("dbg_w2", [1, D]), ("dbg_s", [1, TPC]),
            ("dbg_av", [P, KD]), ("dbg_x1", [P, KD, TT]),
            ("dbg_wd", [P, KD, E]), ("dbg_gl", [P, KD, E]), ("dbg_gp", [P, KD, E]),
            ("dbg_r2", [P, KD, TT]),
            ("dbg_x1b", [P, KD, TT]), ("dbg_r2b", [P, KD, TT]),
            ("dbg_wdb", [P, KD, E]),
        ]:
            dbg[name] = nc.dram_tensor(name, shape, F32, kind="ExternalOutput")
        dbg["dbg_h0"] = nc.dram_tensor("dbg_h0", [P, TH], BF16, kind="ExternalOutput") \
            if debug else None

    ctx = contextlib.ExitStack()
    with ctx:
        sbt = lambda name, shape, dt: ctx.enter_context(nc.sbuf_tensor(name, shape, dt))
        pst = lambda name: ctx.enter_context(nc.psum_tensor(name, [P, TH], F32))
        mks = lambda name: Sem(ctx.enter_context(nc.semaphore(name)), name)

        xtq = sbt("xtq", [P, 4, 4, TT], BF16)
        acol = sbt("acol", [P, 16], F32)
        bcol = sbt("bcol", [P, 16], BF16)
        xf = sbt("xf", [P, KD, TT], F32)
        x1bf = sbt("x1bf", [P, KD, TT], BF16)
        w1b = sbt("w1b", [P, 2, KD, 2 * H], BF16)
        w2b = sbt("w2b", [P, 2, JH, D], BF16)
        actb = sbt("actb", [P, JH, TT], BF16)
        wbc = sbt("wbc", [P, 2, TT], BF16)
        attbc = sbt("attbc", [P, TPC], BF16)
        silu_s = sbt("silu_s", [P, 2, TH], BF16)
        yw_s = sbt("yw_s", [P, 2, TH], BF16)
        rbf = sbt("rbf", [P, 2, TH], BF16)
        sqs = sbt("sqs", [P, 2, TH], BF16)
        gsc = sbt("gsc", [P, KD, E], F32)
        wdtm = sbt("wdtm", [P, KD, E], F32)
        mxt = sbt("mxt", [P, KD, 1], F32)
        mxt2 = sbt("mxt2", [P, KD, 1], F32)
        wfm = sbt("wfm", [E, TT], BF16)
        row32 = sbt("row32", [1, 4096], F32)
        rowbf = sbt("rowbf", [1, TPC], BF16)
        m8a = sbt("m8a", [KD + 1, P], F32)
        m8bf = sbt("m8bf", [KD, P], BF16)
        cols = sbt("cols", [P, 9 * KD], F32)
        colbf = sbt("colbf", [P, 8 * KD], BF16)
        tiny = sbt("tiny", [1, 16], F32)
        idsb = sbt("idsb", [P, P], F32)
        idbsb = sbt("idbsb", [P, P], BF16)
        onecsb = sbt("onecsb", [P, 1], BF16)
        onersb = sbt("onersb", [1, P], BF16)
        onersb32 = sbt("onersb32", [1, P], F32)
        gwsb = sbt("gwsb", [P, KD, E], F32)
        dbgst = sbt("dbgst", [P, TH], BF16) if debug else None

        C_U, C_M1, C_M2, C_AA, C_W2, C_AV, C_WCV, C_TMP, C_ED = \
            0, 1, 2, 3, 4, 5, 6, 7, 8
        csl = lambda w, f=None: cols[:, w * KD:(w + 1) * KD] if f is None \
            else cols[:, w * KD + f: w * KD + f + 1]
        cbsl = lambda w, f=None: colbf[:, w * KD:(w + 1) * KD] if f is None \
            else colbf[:, w * KD + f: w * KD + f + 1]

        seg = lambda a, n: row32[:, a:a + n]
        glsb = attbc.ap().bitcast(F32)[0:E, :]

        pb = [pst(f"pb{i}") for i in range(8)]

        sT, sV, sS = mks("sT"), mks("sV"), mks("sS")
        dP, dG, dO = mks("dP"), mks("dG"), mks("dO")
        dP2, dO2 = mks("dP2"), mks("dO2")
        dX0, dX1 = mks("dX0"), mks("dX1")
        dE = [mks(f"dE{i}") for i in range(4)]
        dXB, dQ = mks("dXB"), mks("dQ")
        dAB = mks("dAB")
        dFB = [mks("dFB0"), mks("dFB1")]
        dWB = [mks("dWB0"), mks("dWB1")]
        dF = [mks(f"dF{i}") for i in range(8)]
        dW1 = [mks("dW10"), mks("dW11")]
        dW2 = [mks("dW20"), mks("dW21")]
        S_ = Script()

        def T(fn, waits=(), inc=False):
            return S_.emit("T", fn, waits, (sT, 1) if inc else None)

        def V(fn, waits=(), inc=True):
            return S_.emit("V", fn, waits, (sV, 1) if inc else None)

        def SS(fn, waits=(), inc=True):
            return S_.emit("S", fn, waits, (sS, 1) if inc else None)

        def dma_sp(dst, src, waits=()):
            return S_.emit("P", lambda e, d=dst, s=src: e.dma_start(out=d, in_=s),
                           waits, (dP, 16))

        def dma_out(dst, src, waits=()):
            return S_.emit("P", lambda e, d=dst, s=src: e.dma_start(out=d, in_=s),
                           waits, (dO, 16))

        def dma_sp2(dst, src, waits=()):
            return S_.emit("P", lambda e, d=dst, s=src: e.dma_start(out=d, in_=s),
                           waits, (dP2, 16))

        def dma_out2(dst, src, waits=()):
            return S_.emit("P", lambda e, d=dst, s=src: e.dma_start(out=d, in_=s),
                           waits, (dO2, 16))

        def dma_s(dst, src, sem, waits=()):
            return S_.emit("S", lambda e, d=dst, s=src: e.dma_start(out=d, in_=s),
                           waits, (sem, 16))

        def bcast_dma(dst_2d, src_row, dram_t, sem, waits=()):
            n = src_row.shape[-1]
            t1_ = dma_s(dram_t.ap()[0:1, 0:n], src_row, sem, waits=waits)
            return dma_s(dst_2d, dram_t.ap()[0:1, 0:n].broadcast_to((P, n)),
                         sem, waits=[t1_])

        def dma_g(dst, src, waits=(), sem=None):
            return S_.emit("G", lambda e, d=dst, s=src: e.dma_start(out=d, in_=s),
                           waits, (sem if sem is not None else dG, 16))

        bank_tok = [None] * 8

        def bopen(i):
            w = bank_tok[i]
            bank_tok[i] = None
            return [w] if w is not None else []

        def bclose(i, tok):
            bank_tok[i] = tok

        t_id32 = dma_sp(idsb[:, :], id32[:, :])
        t_idbf = dma_sp(idbsb[:, :], idbf[:, :])
        t_onec = dma_sp(onecsb[:, :], onec[:, :])
        t_oner = dma_sp(onersb[:, :], oner[:, :])
        t_oner32 = dma_sp(onersb32[:, :], oner32[:, :])
        t_gw = dma_sp(gwsb[:, :, :], gwT.ap().rearrange("(k p) e -> p k e", p=P))
        t_wcv = dma_sp(m8a[0:KD, :], wconv.ap().rearrange("o (k p) -> (o k) p", p=P))
        xr = xT.ap().rearrange("(k p) t -> p k t", p=P)
        t_xf = []
        qkv_slot = w1b[:, 1, :, 0:D]
        t_wq = dma_g(qkv_slot, wqn.ap().rearrange("(k p) d -> p k d", p=P), sem=dQ)

        seq = [(tt, i) for tt in range(NTT) for i in range(NE)]
        w1_par = {}
        w2_par = {}
        for n, key in enumerate(seq):
            w1_par[key] = n % 2
            w2_par[key] = n % 2

        def w1_src(i):
            if i < E:
                return w1t.ap()[i].rearrange("(k p) h -> p k h", p=P)
            return (ws1a if i == E else ws1b).ap().rearrange("(k p) h -> p k h", p=P)

        def w2_src(i):
            if i < E:
                return w2t.ap()[i].rearrange("(j p) d -> p j d", p=P)
            return (ws2a if i == E else ws2b).ap().rearrange("(j p) d -> p j d", p=P)

        t_w1 = {}
        t_w2 = {}
        w1_free = {0: None, 1: None}
        w2_free = {0: None, 1: None}

        def emit_w1_load(key):
            if key in t_w1:
                return
            par = w1_par[key]
            t_w1[key] = dma_g(w1b[:, par, :, :], w1_src(key[1]),
                              waits=[w1_free[par]] if w1_free[par] else [],
                              sem=dW1[par])

        def emit_w2_load(key):
            if key in t_w2:
                return
            par = w2_par[key]
            t_w2[key] = dma_g(w2b[:, par, :, :], w2_src(key[1]),
                              waits=[w2_free[par]] if w2_free[par] else [],
                              sem=dW2[par])


        def row_matvec(out_row, lhs_col_bf, rhs3, pre_waits, scale_ap=None):
            toks = []
            last = None
            for hf in range(2):
                w0 = bopen(7)
                for k in range(KD):
                    w = list(pre_waits) + w0 if (hf == 0 and k == 0) else \
                        (w0 if k == 0 else [])
                    last = T(lambda e, hf=hf, k=k: e.matmul(
                        pb[7][0:1, :], lhs_col_bf[:, k:k + 1],
                        rhs3[:, k, TH * hf:TH * (hf + 1)],
                        start=(k == 0), stop=(k == KD - 1), skip_group_check=True),
                        waits=w, inc=(k == KD - 1))
                if scale_ap is None:
                    tok = V(lambda e, hf=hf: e.tensor_copy(
                        out_row[:, TH * hf:TH * (hf + 1)], pb[7][0:1, :]),
                        waits=[last])
                else:
                    tok = V(lambda e, hf=hf: e.tensor_scalar(
                        out_row[:, TH * hf:TH * (hf + 1)], pb[7][0:1, :],
                        scale_ap, None, OP.mult), waits=[last])
                bclose(7, tok)
                toks.append(tok)
            return toks, last

        def row_over_tokens(out_row, lhs_col_bf, pre_waits):
            toks = []
            for tq in range(4):
                bi = 6 if tq % 2 == 0 else 7
                w0 = bopen(bi)
                last = None
                for k in range(KD):
                    w = list(pre_waits) + w0 if (tq == 0 and k == 0) else \
                        (w0 if k == 0 else [])
                    last = T(lambda e, tq=tq, k=k, bi=bi: e.matmul(
                        pb[bi][0:1, :], lhs_col_bf[:, k:k + 1],
                        xbf[:, k, TH * tq:TH * (tq + 1)],
                        start=(k == 0), stop=(k == KD - 1), skip_group_check=True),
                        waits=w, inc=(k == KD - 1))
                tok = V(lambda e, tq=tq, bi=bi: e.tensor_copy(
                    out_row[:, TH * tq:TH * (tq + 1)], pb[bi][0:1, :]), waits=[last])
                bclose(bi, tok)
                toks.append(tok)
            return toks

        def bcast_row(dst, src_bf_row, n, pre_waits):
            toks = []
            for q in range(n // TH):
                bi = 6 if q % 2 == 0 else 7
                w0 = bopen(bi)
                mmt = T(lambda e, q=q, bi=bi: e.matmul(
                    pb[bi][:, :], onersb[:, :], src_bf_row[:, TH * q:TH * (q + 1)],
                    start=True, stop=True, skip_group_check=True),
                    waits=list(pre_waits) + w0 if q == 0 else w0, inc=True)
                tok = V(lambda e, q=q, bi=bi: e.tensor_copy(
                    dst[:, TH * q:TH * (q + 1)], pb[bi][:, :]), waits=[mmt])
                bclose(bi, tok)
                toks.append(tok)
            return toks

        def row_to_col_bf(row_f32, col_bf_dst, pre_waits):
            tcast = V(lambda e: e.tensor_copy(rowbf[:, 0:D], row_f32), waits=pre_waits)
            td1 = dma_sp(drbf[:, :], rowbf[:, 0:D], waits=[tcast])
            td2 = dma_sp(m8bf[:, :], drbf.ap().rearrange("o (k p) -> (o k) p", p=P),
                         waits=[td1])
            w0 = bopen(6)
            ttr = T(lambda e: e.matmul(
                pb[6].ap().bitcast(BF16)[:, 0:KD], m8bf[:, :], idbsb[0:KD, 0:KD],
                is_transpose=True, skip_group_check=True),
                waits=[td2, t_idbf] + w0, inc=True)
            tok = V(lambda e: e.tensor_copy(col_bf_dst,
                                            pb[6].ap().bitcast(BF16)[:, 0:KD]),
                    waits=[ttr])
            bclose(6, tok)
            return tok

        tok_eps = V(lambda e: e.memset(tiny[:, 10:11], float(EPS)))
        w0 = bopen(6)
        t1 = T(lambda e: e.matmul(pb[6][:, 0:KD], m8a[0:KD, :], idsb[0:KD, 0:KD],
                                  is_transpose=True, skip_group_check=True),
               waits=[t_wcv, t_id32] + w0, inc=True)
        tok_wcv = V(lambda e: e.tensor_copy(cbsl(C_WCV), pb[6][:, 0:KD]), waits=[t1])
        bclose(6, tok_wcv)

        w0 = bopen(7)
        tok_u = None
        for k in range(KD):
            for cc in range(KD):
                w = ([t_wq, tok_wcv] + w0) if (k == 0 and cc == 0) else []
                tok_u = T(lambda e, k=k, cc=cc: e.matmul(
                    pb[7][:, k:k + 1], qkv_slot[:, cc, P * k:P * (k + 1)],
                    cbsl(C_WCV, cc),
                    start=(cc == 0), stop=(cc == KD - 1), skip_group_check=True),
                    waits=w, inc=(k == KD - 1 and cc == KD - 1))
        tok_ubf = V(lambda e: e.tensor_scalar(cbsl(C_U), pb[7][:, 0:KD], SCALE, None,
                                              OP.mult), waits=[tok_u])
        bclose(7, tok_ubf)
        if debug:
            vv = V(lambda e: e.tensor_scalar(csl(C_U), cbsl(C_U), 1.0, None, OP.mult),
                   waits=[tok_ubf])
            dma_out(dbg["dbg_u"][:, :], csl(C_U), waits=[vv])

        t_wk = dma_g(qkv_slot, wkTd.ap().rearrange("(k p) d -> p k d", p=P),
                     waits=[tok_u], sem=dQ)

        def finish_early(tok):
            dma_out(out_fm.ap().rearrange("(k p) t -> p k t", p=P)[:, :, 0:TT],
                    xf[:, :, :], waits=[tok, t_xf[0]])
            S_.emit("P", lambda e: None, [(dO, dO.c)], None)
            S_.check()

        w0 = bopen(6)
        t_utr = T(lambda e: e.matmul(
            pb[6].ap().bitcast(BF16)[0:KD, 0:P], cbsl(C_U), idbsb[:, :],
            is_transpose=True, skip_group_check=True),
            waits=[tok_ubf, t_idbf] + w0, inc=True)
        t_ucp = V(lambda e: e.tensor_copy(m8bf[:, :],
                                          pb[6].ap().bitcast(BF16)[0:KD, 0:P]),
                  waits=[t_utr])
        bclose(6, t_ucp)
        t_ub1 = dma_g(drbf.ap().rearrange("o (k p) -> (o k) p", p=P), m8bf[:, :],
                      waits=[t_ucp], sem=dAB)
        t_ub2 = dma_g(attbc[:, 0:TT], drbf.ap()[0:1, 0:D].broadcast_to((P, D)),
                      waits=[t_ub1], sem=dAB)

        NH = 16
        NPAR = 4
        xq = xtp.ap().rearrange("(c p) d -> p c d", p=P)
        xqb = xtpb.ap().rearrange("(c p) d -> p c d", p=P)
        stt_dump = actb[:, 0, :]

        def tp_pass(bc_off, bc_tok, use_exp, st_prev, mm_prev, src=None,
                    src_rdy=None, echo=False, resident=()):
            st_last = dict(st_prev)
            mm_last = dict(mm_prev)
            cast_last = {}
            echo_toks = {}
            w6 = bopen(6)
            w7 = bopen(7)
            w5 = bopen(5) if use_exp else []
            lt = None
            mm = None
            order = list(resident) + [h for h in range(NH) if h not in resident]
            for hi, h in enumerate(order):
                par = h % NPAR
                if h in resident:
                    t_d0 = t_d1 = None
                else:
                    w = []
                    if st_last.get(par) is not None:
                        w.append(st_last[par])
                    if mm_last.get(par) is not None:
                        w.append(mm_last[par])
                    if echo and hi >= NPAR:
                        w.append(echo_toks[order[hi - NPAR]])
                    if src_rdy is not None:
                        w.append(src_rdy[h])
                    sq = xq if src is None else src
                    t_d0 = dma_g(xtq[:, par, 0:2, :], sq[:, 4 * h:4 * h + 2, :],
                                 waits=w, sem=dF[2 * par])
                    t_d1 = dma_g(xtq[:, par, 2:4, :], sq[:, 4 * h + 2:4 * h + 4, :],
                                 waits=list(w), sem=dF[2 * par + 1])
                    if echo:
                        echo_toks[h] = dma_g(xqb[:, 4 * h:4 * h + 4, :],
                                             xtq[:, par, :, :],
                                             waits=[t_d0, t_d1], sem=dE[par])
                for c4 in range(4):
                    w = [t_d0] if c4 == 0 else ([t_d1] if c4 == 2 else [])
                    if c4 == 0 and hi == 0:
                        w.append(bc_tok)
                    if c4 == 0 and h in resident:
                        w.append(mm_last.get(par))
                    if c4 == 0 and cast_last.get(par) is not None:
                        w.append(cast_last[par])
                    lt = V(lambda e, par=par, c4=c4, bc_off=bc_off:
                           e.scalar_tensor_tensor(
                               stt_dump, xtq[:, par, c4, :], 1.0,
                               attbc[:, bc_off:bc_off + TT], OP.mult, OP.mult,
                               accum_out=acol[:, 4 * par + c4:4 * par + c4 + 1]),
                           waits=w)
                st_last[par] = lt
                if use_exp:
                    ct = SS(lambda e, par=par: e.activation(
                        bcol[:, 4 * par:4 * par + 4], acol[:, 4 * par:4 * par + 4],
                        AF.Exp), waits=[lt])
                else:
                    ct = V(lambda e, par=par: e.tensor_copy(
                        bcol[:, 4 * par:4 * par + 4], acol[:, 4 * par:4 * par + 4]),
                        waits=[lt])
                cast_last[par] = ct
                for c4 in range(4):
                    for h2 in range(2):
                        bi = 6 if h2 == 0 else 7
                        w = [ct] if (c4 == 0 and h2 == 0) else []
                        if hi == 0 and c4 == 0:
                            w = w + (w6 if h2 == 0 else w7)
                        mm = T(lambda e, par=par, c4=c4, h2=h2, bi=bi, h=h:
                               e.matmul(
                                   pb[bi][0:1, 0:TH],
                                   bcol[:, 4 * par + c4:4 * par + c4 + 1],
                                   xtq[:, par, c4, TH * h2:TH * (h2 + 1)],
                                   start=(hi == 0 and c4 == 0),
                                   stop=(hi == NH - 1 and c4 == 3),
                                   skip_group_check=True),
                               waits=w,
                               inc=(not use_exp and c4 == 3 and h2 == 1))
                if use_exp:
                    zw = list(w5) + [t_onec] if hi == 0 else []
                    mm = T(lambda e, par=par, hi=hi: e.matmul(
                        pb[5][0:1, 0:4], onecsb[:, :], bcol[:, 4 * par:4 * par + 4],
                        start=(hi == 0), stop=(hi == NH - 1), skip_group_check=True),
                        waits=zw, inc=True)
                mm_last[par] = mm
            return st_last, mm_last, lt, mm, echo_toks

        st1, mml1, tok_p1_stt, tok_p1_mm, ech1 = tp_pass(0, t_ub2, False,
                                                         {}, {})
        t_xf.append((dma_g(xf[:, 0:4, :], xr[:, 0:4, 0:TT], sem=dX0),
                     dma_g(xf[:, 4:8, :], xr[:, 4:8, 0:TT], sem=dX1)))
        emit_w1_load((0, 0))
        emit_w2_load((0, 0))
        emit_w2_load((0, 1))
        ev6 = V(lambda e: e.tensor_copy(seg(0, TH), pb[6][0:1, 0:TH]),
                waits=[tok_p1_mm])
        ev7 = V(lambda e: e.tensor_copy(seg(TH, TH), pb[7][0:1, 0:TH]),
                waits=[tok_p1_mm])
        bclose(6, ev6)
        bclose(7, ev7)
        tok_m1c = row_to_col_bf(seg(0, D), cbsl(C_M1), [ev6, ev7])
        tok_m1g = tok_m1c
        if debug:
            vm1 = V(lambda e: e.tensor_copy(csl(C_M1), cbsl(C_M1)), waits=[tok_m1c])
            dma_out(dbg["dbg_m1"][:, :], csl(C_M1), waits=[vm1])

        r1_row = seg(2048, D)
        r1toks, tok_r1last = row_matvec(r1_row, cbsl(C_M1), qkv_slot, [tok_m1g, t_wk])
        t_wv = dma_g(qkv_slot, wvn.ap().rearrange("(k p) d -> p k d", p=P),
                     waits=[tok_r1last], sem=dQ)
        tok_mx = V(lambda e: e.reduce_max(tiny[:, 0:1], r1_row, axis=AX), waits=r1toks)
        tok_ng = V(lambda e: e.tensor_scalar(tiny[:, 1:2], tiny[:, 0:1], -1.0, None,
                                             OP.mult), waits=[tok_mx])
        tok_ex = SS(lambda e: e.activation(r1_row, r1_row, AF.Exp, bias=tiny[0:1, 1:2],
                                           scale=1.0, accum_out=tiny[:, 2:3]),
                    waits=[tok_ng])
        tok_rc = V(lambda e: e.reciprocal(tiny[:, 3:4], tiny[:, 2:3]), waits=[tok_ex])
        tok_aa = V(lambda e: e.tensor_scalar(r1_row, r1_row, tiny[0:1, 3:4], None,
                                             OP.mult), waits=[tok_rc])
        if debug:
            dma_out(dbg["dbg_aa"][:, :], r1_row, waits=[tok_aa])
        tok_aac = row_to_col_bf(r1_row, cbsl(C_AA), [tok_aa])

        w2_row = seg(2048, D)
        w2toks, tok_w2last = row_matvec(w2_row, cbsl(C_AA), qkv_slot, [tok_aac, t_wv])
        if debug:
            dma_out(dbg["dbg_w2"][:, :], w2_row, waits=w2toks)
        t_wvt = dma_g(qkv_slot, wvTd.ap().rearrange("(k p) d -> p k d", p=P),
                      waits=[tok_w2last], sem=dQ)
        t_w2cast = V(lambda e: e.tensor_copy(rowbf[:, 0:D], w2_row),
                     waits=w2toks)
        w2b_toks = bcast_row(attbc[:, TT:2 * TT], rowbf[:, 0:D], D, [t_w2cast])
        tok_w2b2 = w2b_toks[-1]

        st2, mml2, tok_p2_stt, tok_p2_mm, _ = tp_pass(TT, tok_w2b2, True,
                                                      st1, mml1,
                                                      resident=(12, 13, 14, 15))
        ev6b = V(lambda e: e.tensor_copy(seg(0, TH), pb[6][0:1, 0:TH]),
                 waits=[tok_p2_mm])
        ev7b = V(lambda e: e.tensor_copy(seg(TH, TH), pb[7][0:1, 0:TH]),
                 waits=[tok_p2_mm])
        bclose(6, ev6b)
        bclose(7, ev7b)
        tok_z = V(lambda e: e.reduce_sum(tiny[:, 6:7], pb[5][0:1, 0:4], axis=AX),
                  waits=[tok_p2_mm])
        bclose(5, tok_z)
        tok_m2c = row_to_col_bf(seg(0, D), cbsl(C_M2), [ev6b, ev7b])

        tln = SS(lambda e: e.activation(tiny[:, 7:8], tiny[:, 6:7], AF.Ln),
                 waits=[tok_z])
        tnl = V(lambda e: e.tensor_scalar(tiny[:, 8:9], tiny[:, 7:8], -1.0, None,
                                          OP.mult), waits=[tln])
        tok_rz = SS(lambda e: e.activation(tiny[:, 9:10], tiny[:, 8:9], AF.Exp),
                    waits=[tnl])
        tok_m2g = tok_m2c

        av_row = seg(2048, D)
        avtoks, tok_avlast = row_matvec(av_row, cbsl(C_M2), qkv_slot,
                                        [tok_m2g, t_wvt, tok_rz],
                                        scale_ap=tiny[0:1, 9:10])
        t_av1 = dma_sp(drow[:, :], av_row, waits=avtoks)
        t_av2 = dma_sp(m8a[0:KD, :], drow.ap().rearrange("o (k p) -> (o k) p", p=P),
                       waits=[t_av1])
        w0 = bopen(6)
        tk = T(lambda e: e.matmul(pb[6][:, 0:KD], m8a[0:KD, :], idsb[0:KD, 0:KD],
                                  is_transpose=True, skip_group_check=True),
               waits=[t_av2] + w0, inc=True)
        tok_avc = V(lambda e: e.tensor_copy(csl(C_AV), pb[6][:, 0:KD]), waits=[tk])
        bclose(6, tok_avc)
        if debug:
            dma_out(dbg["dbg_av"][:, :], csl(C_AV), waits=[tok_avc])

        if globals().get('DBG2', False):
            dma_out(dbg["d2_tiny"][:, :], tiny[:, :], waits=[tok_rz])
            dma_out(dbg["d2_m2"][:, :], csl(C_M2), waits=[tok_m2g])
            dma_out(dbg["d2_m1"][:, :], csl(C_M1), waits=[tok_m2g])
            dma_out(dbg["d2_av"][:, :], csl(C_AV), waits=[tok_avc])
        w1_free[1] = tok_avlast
        emit_w1_load((0, 1))

        slot_tok = {0: None, 1: None}
        su_tok = {0: None, 1: None}
        x1bf_free = None
        wfm_free = None
        wrow_free = None
        wbc_free = {0: None, 1: None}
        act_free = None

        def layer_norm(pre_tok, want_x1bf, extra_first=(), attbc_free=()):
            pre_tok_per_k = pre_tok
            """stats + apply on xf (contents final per pre_tok_per_k). a|b broadcast
            goes through attbc.  Returns per-chunk apply tokens."""
            w6 = bopen(6)
            w7 = bopen(7)
            last_mm = {}
            for k in range(KD):
                for th in range(2):
                    sl = (2 * k + th) % 2
                    pk = pre_tok_per_k[k]
                    w = [pk[th] if isinstance(pk[0], tuple) else pk]
                    if k == 0 and th == 0:
                        w += list(extra_first)
                    if 2 * k + th < 2:
                        if slot_tok[sl] is not None:
                            w.append(slot_tok[sl])
                    else:
                        w.append(last_mm[2 * k + th - 2])
                    tc = V(lambda e, k=k, th=th, sl=sl: e.tensor_copy(
                        rbf[:, sl, :], xf[:, k, TH * th:TH * (th + 1)]), waits=w)
                    tq = V(lambda e, sl=sl: e.tensor_tensor(
                        sqs[:, sl, :], rbf[:, sl, :], rbf[:, sl, :], OP.mult),
                        waits=[tc])
                    bi = 6 if th == 0 else 7
                    wm = [tc, t_onec]
                    if k == 0:
                        wm += w6 if th == 0 else w7
                    T(lambda e, sl=sl, bi=bi, k=k: e.matmul(
                        pb[bi][0:1, 0:TH], onecsb[:, :], rbf[:, sl, :],
                        start=(k == 0), stop=(k == KD - 1), skip_group_check=True),
                        waits=wm, inc=False)
                    lt = T(lambda e, sl=sl, bi=bi, k=k: e.matmul(
                        pb[bi][32:33, 0:TH], onecsb[:, :], sqs[:, sl, :],
                        start=(k == 0), stop=(k == KD - 1), skip_group_check=True),
                        waits=[tq], inc=True)
                    last_mm[2 * k + th] = lt
                    slot_tok[sl] = lt
            tok_stats = last_mm[2 * KD - 1]
            ev1 = V(lambda e: e.tensor_copy(seg(0, TH), pb[6][0:1, 0:TH]),
                    waits=[tok_stats])
            ev2 = V(lambda e: e.tensor_copy(seg(TH, TH), pb[7][0:1, 0:TH]),
                    waits=[tok_stats])
            ev3 = V(lambda e: e.tensor_copy(seg(1024, TH), pb[6][32:33, 0:TH]),
                    waits=[tok_stats])
            ev4 = V(lambda e: e.tensor_copy(seg(1024 + TH, TH), pb[7][32:33, 0:TH]),
                    waits=[tok_stats])
            bclose(6, ev3)
            bclose(7, ev4)
            f3 = V(lambda e: e.scalar_tensor_tensor(
                seg(2048, TT), seg(0, TT), 1.0 / D, seg(0, TT),
                OP.mult, OP.mult), waits=[ev1, ev2])
            f4 = V(lambda e: e.tensor_tensor(seg(1024, TT), seg(1024, TT),
                                             seg(2048, TT), OP.subtract),
                   waits=[ev3, ev4, f3])
            f5 = SS(lambda e: e.activation(seg(1024, TT), seg(1024, TT), AF.Sqrt,
                                           bias=tiny[0:1, 10:11], scale=1.0 / D),
                    waits=[f4, tok_eps])
            f6 = V(lambda e: e.reciprocal(seg(2048, TT), seg(1024, TT)), waits=[f5])
            f7 = V(lambda e: e.scalar_tensor_tensor(
                seg(3072, TT), seg(0, TT), -1.0 / D, seg(2048, TT),
                OP.mult, OP.mult), waits=[f6])
            fab = V(lambda e: e.tensor_copy(rowbf[:, 0:2 * TT], seg(2048, 2 * TT)),
                    waits=[f6, f7])
            ab_toks = bcast_row(attbc[:, :], rowbf[:, 0:2 * TT], 2 * TT,
                                [fab] + list(attbc_free))
            ap_toks = []
            for k in range(KD):
                t1_ = V(lambda e, k=k: e.tensor_tensor(
                    xf[:, k, :], xf[:, k, :], attbc[:, 0:TT], OP.mult),
                    waits=[ab_toks[1]] if k == 0 else [])
                t2_ = V(lambda e, k=k: e.tensor_tensor(
                    xf[:, k, :], xf[:, k, :], attbc[:, TT:2 * TT], OP.add),
                    waits=[t1_])
                if want_x1bf:
                    w = [t2_]
                    if k == 0 and x1bf_free is not None:
                        w.append(x1bf_free)
                    t3_ = V(lambda e, k=k: e.tensor_copy(x1bf[:, k, :], xf[:, k, :]),
                            waits=w)
                    ap_toks.append(t3_)
                else:
                    ap_toks.append(t2_)
            return ap_toks

        if att_only:
            tok_dump = V(lambda e: e.tensor_copy(xf[:, 0, 0:KD], csl(C_AV)),
                         waits=[tok_avc, t_xf[0]])
            dma_out(out_fm.ap().rearrange("(k p) t -> p k t", p=P)[:, :, 0:TT],
                    xf[:, :, :], waits=[tok_dump])
            S_.emit("P", lambda e: None, [(dO, dO.c)], None)
            S_.check()
            ctx2 = True
        out_dma_tok = None
        attbc_prev = [tok_p2_stt]
        for tt in range(NTT if not att_only else 0):
            if tt > 0:
                prev_l = [None, None]
                lds = []
                for k in range(KD):
                    w = [out_k_toks[k]]
                    if prev_l[k % 2] is not None:
                        w.append(prev_l[k % 2])
                    tkk = (dma_sp if k % 2 == 0 else dma_sp2)(
                        xf[:, k, :], xr[:, k, TT * tt:TT * (tt + 1)], waits=w)
                    prev_l[k % 2] = tkk
                    lds.append(tkk)
                t_xf.append(tuple(lds))
            rpre_tok = {}
            for k in range(KD):
                rpre_tok[k] = V(lambda e, k=k: e.tensor_scalar(
                    xf[:, k, :], xf[:, k, :], csl(C_AV, k), None, OP.add),
                    waits=[t_xf[tt][k] if tt > 0 else t_xf[0][k // 4], tok_avc])
            ap_tok = layer_norm(rpre_tok, True, attbc_free=attbc_prev)
            x1_all = ap_tok[-1]
            if debug:
                nm = "dbg_x1" if tt == 0 else "dbg_x1b"
                for k in range(KD):
                    dma_out(dbg[nm][:, k, :], xf[:, k, :], waits=[x1_all])

            gl_ev = []
            gl_last = None
            for th in range(2):
                bi = 6 if th == 0 else 7
                w0b = bopen(bi)
                for k in range(KD):
                    w = list(w0b) if k == 0 else []
                    if k == 0:
                        w += [t_gw]
                    w += [ap_tok[k]]
                    gl_last = T(lambda e, th=th, k=k, bi=bi: e.matmul(
                        pb[bi][0:E, 0:TH], gwsb[:, k, :],
                        xf[:, k, TH * th:TH * (th + 1)],
                        start=(k == 0), stop=(k == KD - 1), skip_group_check=True),
                        waits=w, inc=(k == KD - 1))
                ev = V(lambda e, th=th, bi=bi: e.tensor_copy(
                    glsb[:, TH * th:TH * (th + 1)], pb[bi][0:E, 0:TH]),
                    waits=[gl_last])
                bclose(bi, ev)
                gl_ev.append(ev)
            gate_T_done = gl_last
            tok_gev = []
            for half in range(2):
                bi = 6 if half == 0 else 7
                w0b = bopen(bi)
                lt = None
                for c4 in range(4):
                    c = half * 4 + c4
                    lt = T(lambda e, c=c, c4=c4, bi=bi: e.matmul(
                        pb[bi][0:P, E * c4:E * (c4 + 1)],
                        glsb[:, P * c:P * (c + 1)], idsb[0:E, 0:E],
                        is_transpose=True,
                        start=(c4 == 0), stop=(c4 == 3), skip_group_check=True),
                        waits=(gl_ev + w0b) if c4 == 0 else [], inc=(c4 == 3))
                ev = V(lambda e, half=half, bi=bi: e.tensor_copy(
                    gsc[:, 4 * half:4 * (half + 1), :].rearrange("p a b -> p (a b)"),
                    pb[bi][:, 0:4 * E]), waits=[lt])
                bclose(bi, ev)
                tok_gev.append(ev)
            g3 = gsc[:, :, :]
            wd3 = wdtm[:, :, :]
            mx = mxt[:, :, :]
            if debug and tt == 0:
                tdl = V(lambda e: e.tensor_copy(wd3, g3), waits=tok_gev)
                dma_out(dbg["dbg_gl"][:, :, :], wd3, waits=[tdl])
            V(lambda e: e.reduce_max(mx, g3, axis=AX), waits=tok_gev)
            tsub = V(lambda e: e.tensor_tensor(g3, g3, mx.broadcast_to((P, KD, E)),
                                               OP.subtract))
            tex = SS(lambda e: e.activation(wd3.rearrange("p a b -> p (a b)"),
                                            g3.rearrange("p a b -> p (a b)"),
                                            AF.Exp), waits=[tsub])
            tsum = V(lambda e: e.reduce_sum(mx, wd3, axis=AX), waits=[tex])
            tln = SS(lambda e: e.activation(mxt2[:, :, :], mx, AF.Ln), waits=[tsum])
            tsub2 = V(lambda e: e.tensor_tensor(
                g3, g3, mxt2[:, :, 0:1].broadcast_to((P, KD, E)), OP.subtract),
                waits=[tln])
            tex2 = SS(lambda e: e.activation(g3.rearrange("p a b -> p (a b)"),
                                             g3.rearrange("p a b -> p (a b)"),
                                             AF.Exp), waits=[tsub2])
            if debug and tt == 0:
                tdp = V(lambda e: e.tensor_copy(wd3, g3), waits=[tex2])
                dma_out(dbg["dbg_gp"][:, :, :], wd3, waits=[tdp])
            V(lambda e: e.reduce_max(mx, g3, axis=AX), waits=[tex2])
            V(lambda e: e.tensor_tensor(wd3, g3, mx.broadcast_to((P, KD, E)),
                                        OP.is_equal))
            V(lambda e: e.tensor_tensor(wd3, g3, wd3, OP.subtract))
            V(lambda e: e.reduce_max(mx, wd3, axis=AX))
            V(lambda e: e.tensor_tensor(wd3, g3, mx.broadcast_to((P, KD, E)),
                                        OP.is_ge))
            tok_wd = V(lambda e: e.tensor_tensor(wd3, g3, wd3, OP.mult))
            if debug:
                dma_out(dbg["dbg_wd" if tt == 0 else "dbg_wdb"][:, :, :], wd3,
                        waits=[tok_wd])
            wfm_toks = []
            for half in range(2):
                bi = 6 if half == 0 else 7
                w0b = bopen(bi)
                lt = None
                for c4 in range(4):
                    c = half * 4 + c4
                    w = list(w0b) if c4 == 0 else []
                    if c4 == 0:
                        w.append(tok_wd)
                        if wfm_free is not None:
                            w.append(wfm_free)
                    lt = T(lambda e, c=c, c4=c4, bi=bi: e.matmul(
                        pb[bi][0:E, P * c4:P * (c4 + 1)], wdtm[:, c, :],
                        idsb[:, :], is_transpose=True,
                        start=(c4 == 0), stop=(c4 == 3), skip_group_check=True),
                        waits=w, inc=(c4 == 3))
                ev = V(lambda e, half=half, bi=bi: e.tensor_copy(
                    wfm[:, TH * half:TH * (half + 1)], pb[bi][0:E, 0:TH]),
                    waits=[lt])
                bclose(bi, ev)
                wfm_toks.append(ev)

            first_eo_add = True
            last_add_kth = {}
            for e_i in range(NE):
                has_w = e_i < E
                key = (tt, e_i)
                par1 = w1_par[key]
                par2 = w2_par[key]
                pos = tt * NE + e_i
                nxt = seq[pos + 1] if pos + 1 < len(seq) else None
                if has_w:
                    ea_ = e_i % 2
                    wv_ = list(wfm_toks)
                    if wbc_free[ea_] is not None:
                        wv_.append(wbc_free[ea_])
                    dma_s(dwbd[ea_].ap()[0:1, :], wfm[e_i:e_i + 1, :], dWB[ea_],
                          waits=wv_)
                    tok_wb = dma_s(
                        wbc[:, ea_, :],
                        dwbd[ea_].ap()[0:1, :].broadcast_to((P, TT)), dWB[ea_])
                last_h = None
                act_last = {0: None, 1: None}
                for u_i, (j, th) in enumerate([(j, th) for j in range(JH)
                                               for th in range(2)]):
                    yb = u_i % 2
                    gb = 2 + u_i % 2
                    wy = bopen(yb)
                    wg = bopen(gb)
                    if u_i == 0:
                        wy = wy + [t_w1[key], x1_all]
                    lty = ltg = None
                    for k in range(KD):
                        lty = T(lambda e, j=j, th=th, k=k, yb=yb, par1=par1: e.matmul(
                            pb[yb][:, :], w1b[:, par1, k, P * j:P * (j + 1)],
                            x1bf[:, k, TH * th:TH * (th + 1)],
                            start=(k == 0), stop=(k == KD - 1),
                            skip_group_check=True),
                            waits=wy if k == 0 else [], inc=(k == KD - 1))
                    for k in range(KD):
                        ltg = T(lambda e, j=j, th=th, k=k, gb=gb, par1=par1: e.matmul(
                            pb[gb][:, :],
                            w1b[:, par1, k, P * (JH + j):P * (JH + j + 1)],
                            x1bf[:, k, TH * th:TH * (th + 1)],
                            start=(k == 0), stop=(k == KD - 1),
                            skip_group_check=True),
                            waits=wg if k == 0 else [], inc=(k == KD - 1))
                    last_h = ltg
                    ssl = u_i % 2
                    wsil = [ltg]
                    if su_tok[ssl] is not None:
                        wsil.append(su_tok[ssl])
                    tsil = SS(lambda e, ssl=ssl, gb=gb: e.activation(
                        silu_s[:, ssl, :], pb[gb][:, :], AF.Silu), waits=wsil)
                    if has_w:
                        tyw = V(lambda e, ssl=ssl, yb=yb, e_i=e_i, th=th:
                                e.tensor_tensor(
                                    yw_s[:, ssl, :], pb[yb][:, :],
                                    wbc[:, e_i % 2, TH * th:TH * (th + 1)], OP.mult),
                                waits=[lty] + ([tok_wb] if u_i == 0 else []))
                        wbc_free[e_i % 2] = tyw
                        wact = [tsil, tyw]
                        if u_i == 0 and act_free is not None:
                            wact.append(act_free)
                        tact = V(lambda e, ssl=ssl, j=j, th=th: e.tensor_tensor(
                            actb[:, j, TH * th:TH * (th + 1)],
                            yw_s[:, ssl, :], silu_s[:, ssl, :], OP.mult), waits=wact)
                        bclose(yb, tyw)
                        bclose(gb, tact)
                    else:
                        wact = [tsil, lty]
                        if u_i == 0 and act_free is not None:
                            wact.append(act_free)
                        tact = V(lambda e, ssl=ssl, j=j, th=th, yb=yb: e.tensor_tensor(
                            actb[:, j, TH * th:TH * (th + 1)],
                            pb[yb][:, :], silu_s[:, ssl, :], OP.mult), waits=wact)
                        bclose(yb, tact)
                        bclose(gb, tact)
                    su_tok[ssl] = tact
                    act_last[th] = tact
                    if debug and tt == 0 and e_i == 0 and j == 0 and th == 0:
                        vv = V(lambda e, yb=yb: e.tensor_copy(
                            dbgst[:, :], pb[yb][:, :]), waits=[lty])
                        dma_out(dbg["dbg_h0"][:, :], dbgst[:, :], waits=[vv])
                w1_free[par1] = last_h
                if e_i == NE - 1 and tt == NTT - 1:
                    pass
                if nxt is not None:
                    emit_w1_load(nxt)
                last_b = None
                for k in range(KD):
                    for th in range(2):
                        eb = 4 + (2 * k + th) % 2
                        w0b = bopen(eb)
                        lt = None
                        for j in range(JH):
                            w = list(w0b) if j == 0 else []
                            if j == 0:
                                w.append(act_last[th])
                                if k == 0 and th == 0:
                                    w.append(t_w2[key])
                            lt = T(lambda e, k=k, th=th, j=j, eb=eb, par2=par2:
                                   e.matmul(
                                       pb[eb][:, :],
                                       w2b[:, par2, j, P * k:P * (k + 1)],
                                       actb[:, j, TH * th:TH * (th + 1)],
                                       start=(j == 0), stop=(j == JH - 1),
                                       skip_group_check=True),
                                   waits=w, inc=(j == JH - 1))
                        wadd = [lt]
                        if first_eo_add:
                            wadd += [gate_T_done, ap_tok[k]]
                        tadd = V(lambda e, k=k, th=th, eb=eb: e.tensor_tensor(
                            xf[:, k, TH * th:TH * (th + 1)],
                            xf[:, k, TH * th:TH * (th + 1)], pb[eb][:, :], OP.add),
                            waits=wadd)
                        bclose(eb, tadd)
                        last_b = lt
                        last_b_add = tadd
                        if e_i == NE - 1:
                            last_add_kth[(k, th)] = tadd
                first_eo_add = False
                act_free = last_b
                w2_free[par2] = last_b
                if nxt is not None:
                    emit_w2_load(nxt)
                if e_i == NE - 1:
                    x1bf_free = last_h

            r2_tok = {k: (last_add_kth[(k, 0)], last_add_kth[(k, 1)])
                      for k in range(KD)}
            if debug:
                nm2 = "dbg_r2" if tt == 0 else "dbg_r2b"
                for k in range(KD):
                    dma_out(dbg[nm2][:, k, :], xf[:, k, :], waits=[last_b_add])
            out_toks = layer_norm(r2_tok, False,
                                  attbc_free=[x1_all, tok_gev[-1]])
            attbc_prev = [out_toks[-1]]
            ofm = out_fm.ap().rearrange("(k p) t -> p k t", p=P)
            prev_o = [None, None]
            out_k_toks = []
            for k in range(KD):
                w = [out_toks[k]]
                if prev_o[k % 2] is not None:
                    w.append(prev_o[k % 2])
                tkk = (dma_out if k % 2 == 0 else dma_out2)(
                    ofm[:, k, TT * tt:TT * (tt + 1)], xf[:, k, :], waits=w)
                prev_o[k % 2] = tkk
                out_k_toks.append(tkk)

        if not att_only:
            S_.emit("P", lambda e: None, [(dO, dO.c), (dO2, dO2.c)], None)
            S_.check()

        def replay(eng_obj, key):
            for fn, waits, inc in S_.ops[key]:
                seen = {}
                for s, v in waits:
                    if s.name in seen:
                        seen[s.name] = (s, max(v, seen[s.name][1]))
                    else:
                        seen[s.name] = (s, v)
                for s, v in seen.values():
                    eng_obj.wait_ge(s.h, v)
                ins = fn(eng_obj)
                if inc is not None:
                    assert ins is not None
                    ins.then_inc(inc[0].h, inc[1])

        with nc.Block() as block:
            @block.tensor
            def _(t):
                replay(t, "T")

            @block.vector
            def _(v):
                replay(v, "V")

            @block.scalar
            def _(s):
                replay(s, "S")

            @block.gpsimd
            def _(g):
                replay(g, "G")

            @block.sync
            def _(sp):
                replay(sp, "P")

    return nc


_NC_CACHE = {}


def _get_nc(debug=False):
    if debug not in _NC_CACHE:
        _NC_CACHE[debug] = build_program(debug)
    return _NC_CACHE[debug]


def _prep_inputs(inputs):
    x = np.ascontiguousarray(np.asarray(inputs["L0_attn_x"], dtype=np.float32))
    w_qkv = np.ascontiguousarray(np.asarray(inputs["L0_attn_w_qkv"], dtype=np.float32))
    w_conv1 = np.ascontiguousarray(
        np.asarray(inputs["L0_attn_w_conv1"], dtype=np.float32))
    gate_w = np.asarray(inputs["L0_ffn_gate_w"], dtype=np.float32)
    W1 = np.asarray(inputs["L0_ffn_W1"], dtype=np.float32)
    W2 = np.asarray(inputs["L0_ffn_W2"], dtype=np.float32)
    Ws1 = np.asarray(inputs["L0_ffn_Ws1"], dtype=np.float32)
    Ws2 = np.asarray(inputs["L0_ffn_Ws2"], dtype=np.float32)

    Ws1T = Ws1.T
    Ws2T = Ws2.T
    shared = {
        "w1t": np.ascontiguousarray(W1.transpose(0, 2, 1)),
        "w2t": np.ascontiguousarray(W2.transpose(0, 2, 1)),
        "ws1a": np.ascontiguousarray(
            np.concatenate([Ws1T[:, 0:H], Ws1T[:, 2 * H:3 * H]], axis=1)),
        "ws1b": np.ascontiguousarray(
            np.concatenate([Ws1T[:, H:2 * H], Ws1T[:, 3 * H:4 * H]], axis=1)),
        "ws2a": np.ascontiguousarray(Ws2T[0:H]),
        "ws2b": np.ascontiguousarray(Ws2T[H:2 * H]),
        "wqn": np.ascontiguousarray(w_qkv[0:D]),
        "wvn": np.ascontiguousarray(w_qkv[2 * D:3 * D]),
        "wkTd": np.ascontiguousarray(w_qkv[D:2 * D].T),
        "wvTd": np.ascontiguousarray(w_qkv[2 * D:3 * D].T),
        "wconv": np.ascontiguousarray(w_conv1.reshape(1, D)),
        "gwT": np.ascontiguousarray(gate_w.T),
        "id32": np.eye(P, dtype=np.float32),
        "idbf": np.eye(P, dtype=ml_dtypes.bfloat16),
        "onec": np.ones((P, 1), dtype=ml_dtypes.bfloat16),
        "oner": np.ones((1, P), dtype=ml_dtypes.bfloat16),
        "oner32": np.ones((1, P), dtype=np.float32),
    }
    xT_b = [np.ascontiguousarray(x[b].T) for b in range(2)]
    xfull_b = [np.ascontiguousarray(x[b]) for b in range(2)]
    in_maps = []
    for c in range(8):
        b, s0 = c // 4, (c % 4) * TPC
        m = dict(shared)
        m["xT"] = np.ascontiguousarray(xT_b[b][:, s0:s0 + TPC])
        m["xtp"] = xfull_b[b]
        in_maps.append(m)
    return in_maps


def kernel(**inputs):
    in_maps = _prep_inputs(inputs)
    nc = _get_nc(False)
    res = run_bass_kernel_spmd(nc, in_maps, core_ids=list(range(8)))
    out = np.empty((2, 8192, D), dtype=np.float32)
    for c in range(8):
        b, s0 = c // 4, (c % 4) * TPC
        out[b, s0:s0 + TPC] = res.results[c]["out_fm"].T
    return out
